# version 10
# baseline (speedup 1.0000x reference)
import math
from contextlib import ExitStack

import numpy as np
import ml_dtypes
import concourse.bass as bass
import concourse.mybir as mybir
from concourse.bass_utils import run_bass_kernel_spmd

F32 = mybir.dt.float32
BF16 = mybir.dt.bfloat16
I32 = mybir.dt.int32
AF = mybir.ActivationFunctionType
ALU = mybir.AluOpType
AX = mybir.AxisListType

B, S, D, H, HD, FF = 4, 4096, 2048, 16, 128, 8192
TOWN = 2048
SCALE = 1.0 / math.sqrt(HD)
EPS_A = 1e-5
EPS_B = 1e-6
NEG = -1.0e9

ENGS = ("sync", "gpsimd", "vector", "scalar", "tensor")
SELF_SYNC = {"vector": "dv", "scalar": "ac", "gpsimd": "gp"}
COMPUTE_SEMS = ("dv", "ac", "pe", "gp")
SEM_OWNER = {"dv": "vector", "ac": "scalar", "pe": "tensor", "gp": "gpsimd"}

ARM_PARAMS = [
    dict(extent=4096,
         sts=[(0, 16, 4, [(0, 0), (1, 1)]), (2048, 16, 4, [(2, 2), (3, 3)])],
         qtiles=[(0, 0), (1, 1), (2, 6), (3, 7)]),
    dict(extent=3072,
         sts=[(0, 16, 4, [(2, 0), (3, 1)]), (2048, 8, 2, [(0, 2), (1, 3)])],
         qtiles=[(0, 2), (1, 3), (2, 4), (3, 5)]),
]


class Prog:

    def __init__(self, nc, ctx):
        self.nc = nc
        self.ctx = ctx
        self.segments = [("common", {e: [] for e in ENGS})]
        self.cnt = {}
        self.sems = {}
        self.stream_eng = {}
        self.arm_state = None
        self.arm_pads = None
        self.scope = "init"

    def mksem(self, name):
        self.sems[name] = self.ctx.enter_context(self.nc.semaphore(name))
        self.cnt[name] = 0

    def mksems(self, names):
        for name in names:
            self.mksem(name)

    def _cur(self):
        seg = self.segments[-1]
        return seg[1] if seg[0] == "common" else seg[1][-1]

    def arms_begin(self):
        assert self.arm_state is None
        self.segments.append(("arms", []))
        self.arm_state = (dict(self.cnt), [])

    def arm(self):
        snap, ends = self.arm_state
        if self.segments[-1][1]:
            ends.append(dict(self.cnt))
        self.cnt = dict(snap)
        self.segments[-1][1].append({e: [] for e in ENGS})

    def drain_arm(self):
        snap, _ = self.arm_state
        for st, eng in list(self.stream_eng.items()):
            v = self.cnt.get(st, 0)
            if v > snap.get(st, 0):
                self.op(eng, lambda e: e.nop(), waits=[(st, v)], _isdma=True)

    def arms_end(self):
        snap, ends = self.arm_state
        ends.append(dict(self.cnt))
        final = {k: max(e.get(k, 0) for e in ends) for k in self.cnt}
        self.arm_pads = [{k: final[k] - e.get(k, 0) for k in final
                          if k in COMPUTE_SEMS} for e in ends]
        self.arm_ends = ends
        self.cnt = final
        self.arm_state = None
        self.segments.append(("common", {e: [] for e in ENGS}))

    def op(self, eng, fn, waits=(), inc=None, incn=1, _isdma=False):
        waits = list(waits)
        if not _isdma and eng in SELF_SYNC:
            own = SELF_SYNC[eng]
            if inc is None:
                inc, incn = own, 1
            assert inc == own, (eng, inc)
            waits.append((own, self.cnt[own]))
        self._cur()[eng].append((tuple(waits), fn, inc, incn, self.scope))
        if inc is not None:
            self.cnt[inc] = self.cnt.get(inc, 0) + incn
            return self.cnt[inc]
        return None

    def dma(self, eng, stream, out, in_, waits=(), **kw):
        if stream not in self.sems:
            self.mksem(stream)
        assert self.stream_eng.setdefault(stream, eng) == eng, stream
        waits = list(waits) + [(stream, self.cnt.get(stream, 0))]
        return self.op(eng, lambda e: e.dma_start(out=out, in_=in_, **kw),
                       waits=waits, inc=stream, incn=16, _isdma=True)

    def c(self, sem):
        return self.cnt.get(sem, 0)

    def emit(self, block, gsel_ap, gsel_ready):
        sems = self.sems

        cur_scope = [None, 0]

        def emit_ops(eng, ops, hw):
            for waits, fn, inc, incn, scope in ops:
                if scope != cur_scope[0]:
                    if cur_scope[0] is not None:
                        self.nc.leave_named_scope(cur_scope[0], cur_scope[1], False)
                    sid, _ = self.nc.enter_named_scope(scope, False)
                    cur_scope[0], cur_scope[1] = scope, sid
                for (s, v) in waits:
                    if v > 0 and hw.get(s, 0) < v:
                        eng.wait_ge(sems[s], v)
                        hw[s] = v
                inst = fn(eng)
                if inc is not None:
                    inst.then_inc(sems[inc], incn)

        for eng_name in ENGS:
            def body(eng, eng_name=eng_name):
                hw = {}
                post_arms = False
                for seg in self.segments:
                    if seg[0] == "common":
                        if post_arms:
                            eng.wait_ge(sems["join"], 5)
                            post_arms = False
                        emit_ops(eng, seg[1][eng_name], hw)
                        continue
                    if hw.get("gsel_ld", 0) < gsel_ready:
                        eng.wait_ge(sems["gsel_ld"], gsel_ready)
                        hw["gsel_ld"] = gsel_ready
                    with eng.register(f"gsel_{eng_name}") as gr:
                        eng.reg_load(gr, gsel_ap)
                        for i in eng.Switch(eng.snap(gr, min_val=0, max_val=1), 2):
                            emit_ops(eng, seg[1][i][eng_name], dict(hw))
                            for sem, pad in self.arm_pads[i].items():
                                if pad > 0 and SEM_OWNER[sem] == eng_name:
                                    eng.wait_ge(sems[sem],
                                                self.arm_ends[i][sem])
                                    eng.sem_inc(sems[sem], pad)
                            eng.sem_inc(sems["join"], 1)
                    hw = {}
                    post_arms = True
                if cur_scope[0] is not None:
                    self.nc.leave_named_scope(cur_scope[0], cur_scope[1], False)
                    cur_scope[0] = None
            getattr(block, eng_name)(body)


def _r3(ap2d, b):
    return ap2d.rearrange("p (a b) -> p a b", b=b)


def build_fused():
    nc = bass.Bass("TRN2", target_bir_lowering=False)
    ap = lambda t: t.ap()
    xb = ap(nc.dram_tensor("xb", [S, D], F32, kind="ExternalInput"))
    xown = ap(nc.dram_tensor("xown", [TOWN, D], F32, kind="ExternalInput"))
    wqT = ap(nc.dram_tensor("wqT", [D, D], F32, kind="ExternalInput"))
    wkT = ap(nc.dram_tensor("wkT", [D, D], F32, kind="ExternalInput"))
    wvT = ap(nc.dram_tensor("wvT", [D, D], F32, kind="ExternalInput"))
    woT = ap(nc.dram_tensor("woT", [D, D], F32, kind="ExternalInput"))
    wfcT = ap(nc.dram_tensor("wfcT", [D, FF], F32, kind="ExternalInput"))
    wpjT = ap(nc.dram_tensor("wpjT", [FF, D], F32, kind="ExternalInput"))
    cosf = ap(nc.dram_tensor("cosf", [HD, S], F32, kind="ExternalInput"))
    sinf = ap(nc.dram_tensor("sinf", [HD, S], F32, kind="ExternalInput"))
    rtc = ap(nc.dram_tensor("rtc", [HD, HD], BF16, kind="ExternalInput"))
    idbi = ap(nc.dram_tensor("idbi", [128, 128], BF16, kind="ExternalInput"))
    idfi = ap(nc.dram_tensor("idfi", [128, 128], F32, kind="ExternalInput"))
    masks = ap(nc.dram_tensor("masks", [128, 4 * 512], F32, kind="ExternalInput"))
    gseli = ap(nc.dram_tensor("gseli", [1, 1], I32, kind="ExternalInput"))

    xout = ap(nc.dram_tensor("xout", [TOWN, D], F32, kind="ExternalOutput"))

    qss = ap(nc.dram_tensor("qss", [H, HD, TOWN], BF16))
    kss = ap(nc.dram_tensor("kss", [H, HD, S], BF16))
    vss = ap(nc.dram_tensor("vss", [H, HD, S], BF16))
    ag_scr = ap(nc.dram_tensor("ag_scr", [D, TOWN], BF16))
    h_scr = ap(nc.dram_tensor("h_scr", [FF, TOWN], BF16))

    with ExitStack() as ctx:
        sb = lambda name, shape, dt: ctx.enter_context(nc.sbuf_tensor(name, shape, dt))
        psa = lambda name, shape, dt: ctx.enter_context(nc.psum_tensor(name, shape, dt))

        rt_s = sb("rt_s", [HD, HD], BF16)
        idb_s = sb("idb_s", [128, 128], BF16)
        idf_s = sb("idf_s", [128, 128], F32)
        gs_s = sb("gs_s", [1, 1], I32)
        ones1f = sb("ones1f", [1, 128], F32)
        onescol_f = sb("onescol_f", [128, 1], F32)
        onescol_b = sb("onescol_b", [128, 1], BF16)
        epsa_t = sb("epsa_t", [128, 1], F32)
        epsb_t = sb("epsb_t", [128, 1], F32)
        gdump = sb("gdump", [128, 1024], F32)
        gtmp = sb("gtmp", [128, 1], F32)
        gacc = sb("gacc", [128, 2], F32)
        gsc = sb("gsc", [1, 8], F32)
        gfc_b = sb("gfc_b", [128, 1], F32)
        gpj_b = sb("gpj_b", [128, 1], F32)
        gfcinv_b = sb("gfcinv_b", [128, 1], F32)
        gpjinv_b = sb("gpjinv_b", [128, 1], F32)
        thr_fc = sb("thr_fc", [128, 1], F32)
        thr_pj = sb("thr_pj", [128, 1], F32)
        stats = sb("stats", [128, 96], F32)
        mv = sb("mv", [128, 2], F32)
        rstd = sb("rstd", [128, 1], F32)
        xn_tok = sb("xn_tok", [128, D], BF16)
        raw = [sb(f"raw{i}", [128, 512], BF16) for i in range(2)]
        t1b = [sb("t1b0", [128, 512], F32)] * 2
        t2b = [sb("t2b0", [128, 512], F32)] * 2
        stg = [sb(f"stg{i}", [128, 512], BF16) for i in range(4)]
        mx8 = sb("mx8", [1, 16], F32)
        sc1 = sb("sc1", [1, 4], F32)
        negb = [sb(f"negb{i}", [128, 1], F32) for i in range(2)]
        rs_sb = sb("rs_sb", [1, 512], F32)
        r_sb = sb("r_sb", [128, 512], F32)
        ost = [sb(f"ost{i}", [128, 512], BF16) for i in range(2)]
        gq = sb("gq", [128, 1], F32)
        s1v = sb("s1v", [128, 1], F32)
        s2v = sb("s2v", [128, 1], F32)
        hgel = [sb(f"hgel{i}", [128, 512], BF16) for i in range(2)]
        maxh = sb("maxh", [128, 1], F32)
        minh = sb("minh", [128, 1], F32)
        ga = sb("ga", [128, 1], F32)
        gb = sb("gb", [128, 1], F32)
        svec = sb("svec", [128, 16], F32)
        msv = sb("msv", [128, 1], F32)
        msrow = sb("msrow", [1, TOWN], F32)
        crow = sb("crow", [1, 512], F32)
        htile = [sb(f"htile{i}", [128, 512], BF16) for i in range(3)]
        csb = sb("csb", [128, 512], F32)
        o2sb = [sb(f"o2sb{i}", [128, 512], F32) for i in range(2)]

        arena = sb("arena", [128, 49152], BF16)
        ar = arena.ap()
        xn_fm = _r3(ar[:, 0:32768], 2048)
        wbf = [_r3(ar[:, 32768 + i * 8192: 32768 + (i + 1) * 8192], 512)
               for i in range(2)]
        kh = [ar[:, 0:4096], ar[:, 4096:8192]]
        vh = [ar[:, 8192:12288], ar[:, 12288:16384]]
        qh = [ar[:, 16384:18432], ar[:, 18432:20480]]
        vtok = [ar[:, 20480:24576], ar[:, 30720:34816]]
        pb = [ar[:, 24576 + i * 512: 24576 + (i + 1) * 512] for i in range(4)]
        sqb = ar[:, 26624:30720]
        mask_s = ar[:, 45056:49152].bitcast(F32)
        xfm2 = _r3(ar[:, 0:32768], 2048)
        agsl = _r3(ar[:, 32768:40960], 512)
        wob = [_r3(ar[:, 40960 + i * 2048: 40960 + (i + 1) * 2048], 128)
               for i in range(2)]
        tfc = [_r3(ar[:, 45056 + i * 2048: 45056 + (i + 1) * 2048], 128)
               for i in range(2)]
        xh_tok = ar[:, 45056:47104]
        attn_sb = [ar[:, 47104 + i * 1024: 47104 + (i + 1) * 1024].bitcast(F32)
                   for i in range(2)]
        htok = ar[:, 32768:40960]
        htok2 = ar[:, 40960:49152]
        tpj = _r3(ar[:, 0:32768], 512)

        arfa = sb("arfa", [128, 8192], F32)
        cos_s = arfa.ap()[:, 0:2048]
        sin_s = arfa.ap()[:, 2048:4096]
        xt = [arfa.ap()[:, 4096:6144], arfa.ap()[:, 6144:8192]]
        h1t = [arfa.ap()[:, i * 2048:(i + 1) * 2048] for i in range(4)]

        arfb = sb("arfb", [128, 8192], F32)
        xn2 = arfb.ap()[:, 0:2048]
        clipt = arfb.ap()[:, 2048:4096]
        xrow = [arfb.ap()[:, 4096 + i * 2048: 4096 + (i + 1) * 2048]
                for i in range(2)]
        wfs = [_r3(arfb.ap()[:, i * 2048:(i + 1) * 2048], 128)
               for i in range(3)]
        zfc2 = _r3(arfb.ap()[:, 6144:8192], 128)
        zpj2 = _r3(arfb.ap()[:, 4096:8192], 512)
        gdump2 = arfb.ap()[:, 0:1024]
        wps = [_r3(arfa.ap()[:, 0:4096], 512),
               _r3(arfa.ap()[:, 4096:8192], 512),
               _r3(arfb.ap()[:, 0:4096], 512)]
        grow_t = arfb.ap()[0:1, 0:1024]
        grow_a = [arfb.ap()[0:1, 1024 + i * 1024: 1024 + (i + 1) * 1024]
                  for i in range(2)]

        pacc = [psa(f"pacc{i}", [128, 512], F32) for i in range(4)]
        ptr = psa("ptr", [128, 512], BF16)
        psc = [psa(f"psc{i}", [128, 512], F32) for i in range(2)]
        prow = psa("prow", [1, 512], F32)
        ptrf = psc[0]
        pcor = psc[1]

        p = Prog(nc, ctx)
        p.mksems(("dv", "ac", "pe", "gp", "join"))

        gst = {}

        def gpump_reset():
            gst["i"] = 0
            gst["add"] = p.c("dv")
            gst["abs"] = p.c("dv")
            gst["abs0"] = gst["abs1"] = p.c("dv")

        def gpump(n):
            for _ in range(n):
                i = gst["i"]
                if i >= 256:
                    return
                gst["i"] += 1
                gi, si = (0, i) if i < 128 else (1, i - 128)
                src = (wfcT if gi == 0 else wpjT).rearrange("a b -> (a b)")
                blk = src[si * 131072:(si + 1) * 131072].rearrange(
                    "(q n) -> q n", n=1024)
                gpar = i % 2
                gbuf = gdump[:] if gpar == 0 else gdump2
                gstn = "gm" if gpar == 0 else "gm2"
                w = p.dma("scalar", gstn, gbuf, blk,
                          waits=[("dv", gst["abs%d" % gpar])])
                gst["abs%d" % gpar] = gst["abs"] = p.op(
                    "vector",
                    lambda v, gbuf=gbuf: v.tensor_reduce(
                        gtmp[:], gbuf, axis=AX.X, op=ALU.add,
                        apply_absolute_value=True),
                    waits=[(gstn, w)])
                gst["add"] = p.op(
                    "vector",
                    lambda v, gi=gi: v.tensor_tensor(
                        gacc[:, gi:gi + 1], gacc[:, gi:gi + 1], gtmp[:],
                        op=ALU.add))

        with nc.Block() as block:
            p.scope = "c0"
            p.dma("sync", "gsel_ld", gs_s[:], gseli)
            gsel_ready = 16
            p.dma("sync", "consts", rt_s[:], rtc)
            p.dma("sync", "consts", idb_s[:], idbi)
            p.dma("sync", "consts", idf_s[:], idfi)
            consts = p.c("consts")
            p.op("vector", lambda v: v.memset(ones1f[:], 1.0))
            p.op("vector", lambda v: v.memset(onescol_f[:], 1.0))
            p.op("vector", lambda v: v.memset(onescol_b[:], 1.0))
            p.op("vector", lambda v: v.memset(epsa_t[:], EPS_A))
            p.op("vector", lambda v: v.memset(epsb_t[:], EPS_B))
            p.op("vector", lambda v: v.memset(gacc[:], 0.0))
            memsets = p.c("dv")
            gst["ms"] = memsets

            p.arms_begin()
            for g in range(2):
                p.arm()
                prm = ARM_PARAMS[g]
                gpump_reset()
                wr_done = {}
                xn_last = {}
                bank_free = {}
                ost_store = {}

                p.scope = "p2"
                for (t0, n_xt, n_kv_tq, own) in prm["sts"]:
                    p.dma("sync", "cs", cos_s, cosf[:, t0:t0 + 2048],
                          waits=[("gp", p.c("gp")), ("dv", p.c("dv"))])
                    csw = p.dma("sync", "cs", sin_s, sinf[:, t0:t0 + 2048])
                    for ti in range(n_xt):
                        buf = xt[ti % 2]
                        w = p.dma("sync", "sx", buf,
                                  xb[t0 + ti * 128: t0 + (ti + 1) * 128, :],
                                  waits=[("dv", xn_last.get(ti % 2, 0))])
                        for ci in range(4):
                            p.op("vector",
                                 lambda v, buf=buf, ci=ci: v.bn_stats(
                                     stats[:, ci * 6:(ci + 1) * 6],
                                     buf[:, ci * 512:(ci + 1) * 512]),
                                 waits=[("sx", w), ("dv", memsets)])
                        p.op("vector", lambda v: v.bn_aggr(mv[:], stats[:, 0:24]))
                        agg = p.c("dv")
                        p.op("scalar",
                             lambda sc: sc.activation(rstd[:], mv[:, 1:2],
                                                      AF.Ln, bias=epsa_t[:]),
                             waits=[("dv", agg)])
                        p.op("scalar",
                             lambda sc: sc.activation(rstd[:], rstd[:], AF.Exp,
                                                      scale=-0.5))
                        rst = p.c("ac")
                        p.op("vector",
                             lambda v, buf=buf: v.tensor_scalar(
                                 xn_tok[:], buf, mv[:, 0:1], rstd[:],
                                 op0=ALU.subtract, op1=ALU.mult),
                             waits=[("ac", rst)])
                        xn_last[ti % 2] = p.c("dv")
                        for dq in range(4):
                            def trf(t, dq=dq):
                                for j in range(4):
                                    dc = dq * 4 + j
                                    inst = t.transpose(
                                        ptr[:, j * 128:(j + 1) * 128],
                                        xn_tok[:, dc * 128:(dc + 1) * 128],
                                        idb_s[:])
                                return inst
                            p.op("tensor", trf,
                                 waits=[("dv", p.c("dv")), ("consts", consts)],
                                 inc="pe")
                            trp = p.c("pe")
                            p.op("vector",
                                 lambda v, dq=dq, ti=ti: v.tensor_copy(
                                     xn_fm[:, dq * 4:(dq + 1) * 4,
                                           ti * 128:(ti + 1) * 128],
                                     _r3(ptr[:, 0:512], 128)),
                                 waits=[("pe", trp)])
                        gpump(2)
                    xnfm_done = p.c("dv")
                    mm_q = {}
                    for (wsrc, scr, dorope, tqs) in (
                            (wkT, kss, True,
                             [(tq, tq) for tq in range(n_kv_tq)]),
                            (wvT, vss, False,
                             [(tq, tq) for tq in range(n_kv_tq)]),
                            (wqT, qss, True, own)):
                        for qtr in range(4):
                            wb = qtr % 2
                            wl = p.dma("gpsimd", "wl", wbf[wb],
                                       wsrc[:, qtr * 512:(qtr + 1) * 512]
                                       .rearrange("(a p) e -> p a e", p=128),
                                       waits=[("pe", mm_q.get(wb, 0))])
                            for ec4 in range(4):
                                h = qtr * 4 + ec4
                                for (tq, slot) in tqs:
                                    bidx = (ec4 * 4 + tq) % 4
                                    bank = pacc[bidx]

                                    def mmf(t, ec4=ec4, tq=tq, bank=bank,
                                            wb=wb):
                                        for dc in range(16):
                                            inst = t.matmul(
                                                bank[:],
                                                wbf[wb][:, dc,
                                                        ec4 * 128:
                                                        (ec4 + 1) * 128],
                                                xn_fm[:, dc,
                                                      tq * 512:(tq + 1) * 512],
                                                start=(dc == 0),
                                                stop=(dc == 15))
                                        return inst
                                    p.op("tensor", mmf,
                                         waits=[("wl", wl), ("dv", xnfm_done),
                                                ("dv", bank_free.get(bidx, 0))],
                                         inc="pe")
                                    mmn = p.c("pe")
                                    mm_q[wb] = mmn
                                    si = (h * 4 + tq) % 2
                                    so = (h * 4 + tq) % 4
                                    if dorope:
                                        p.op("vector",
                                             lambda v, bank=bank, si=si:
                                             v.tensor_copy(raw[si][:], bank[:]),
                                             waits=[("pe", mmn)])
                                        rawn = p.c("dv")
                                        bank_free[bidx] = rawn
                                        p.op("tensor",
                                             lambda t, si=si: t.matmul(
                                                 psc[1][:], rt_s[:],
                                                 raw[si][:],
                                                 start=True, stop=True),
                                             waits=[("dv", rawn)], inc="pe")
                                        rotn = p.c("pe")
                                        p.op("gpsimd",
                                             lambda gg, si=si, tq=tq:
                                             gg.tensor_tensor(
                                                 t1b[si][:], raw[si][:],
                                                 cos_s[:, tq * 512:
                                                       (tq + 1) * 512],
                                                 op=ALU.mult),
                                             waits=[("dv", rawn), ("cs", csw)])
                                        gpn = p.c("gp")
                                        p.op("vector",
                                             lambda v, si=si, tq=tq:
                                             v.tensor_tensor(
                                                 t2b[si][:], psc[1][:],
                                                 sin_s[:, tq * 512:
                                                       (tq + 1) * 512],
                                                 op=ALU.mult),
                                             waits=[("pe", rotn), ("cs", csw)])
                                        p.op("vector",
                                             lambda v, si=si, so=so:
                                             v.tensor_tensor(
                                                 stg[so][:], t2b[si][:],
                                                 t1b[si][:], op=ALU.add),
                                             waits=[("gp", gpn),
                                                    ("wr", wr_done.get(so, 0))])
                                    else:
                                        p.op("vector",
                                             lambda v, bank=bank, so=so:
                                             v.tensor_copy(stg[so][:],
                                                           bank[:]),
                                             waits=[("pe", mmn),
                                                    ("wr", wr_done.get(so, 0))])
                                        bank_free[bidx] = p.c("dv")
                                    stn = p.c("dv")
                                    dst = (scr[h, :,
                                               slot * 512:(slot + 1) * 512]
                                           if wsrc is wqT else
                                           scr[h, :, t0 + tq * 512:
                                               t0 + (tq + 1) * 512])
                                    wr_done[so] = p.dma(
                                        "gpsimd", "wr", dst, stg[so][:],
                                        waits=[("dv", stn)])
                                gpump(1)

                wr_all = p.c("wr")
                E = prm["extent"]
                nkc = E // 512

                p.scope = "p3"
                mask_ld = p.dma("sync", "kv", mask_s, masks,
                                waits=[("pe", p.c("pe")), ("dv", p.c("dv")),
                                       ("gp", p.c("gp")), ("ac", p.c("ac"))])
                av_end = {}
                ld_h = {}
                negb_d = {}
                vt_d = {}
                out_guard = {}
                sum_guard = 0
                bank_guard = {}
                def loads(h, wpe):
                    hb = h % 2
                    p.dma("sync", "kv", kh[hb][:, 0:E], kss[h][:, 0:E],
                          waits=[("wr", wr_all), ("pe", wpe)])
                    p.dma("sync", "kv", vh[hb][:, 0:E], vss[h][:, 0:E])
                    ld_h[h] = p.dma("sync", "kv", qh[hb], qss[h])

                def bbound(h):
                    hb = h % 2
                    ld = ld_h[h]
                    p.op("scalar",
                         lambda sc: sc.activation(sqb[:, 0:E], kh[hb][:, 0:E],
                                                  AF.Square),
                         waits=[("kv", ld), ("pe", av_end.get(h - 1, 0))])
                    sq1 = p.c("ac")
                    for i in range(nkc):
                        p.op("tensor",
                             lambda t, i=i: t.matmul(
                                 pacc[2][0:1, :], onescol_b[:],
                                 sqb[:, i * 512:(i + 1) * 512],
                                 start=True, stop=True),
                             waits=[("ac", sq1), ("dv", p.c("dv"))], inc="pe")
                        mm = p.c("pe")
                        p.op("vector",
                             lambda v, i=i: v.reduce_max(mx8[:, i:i + 1],
                                                         pacc[2][0:1, :],
                                                         axis=AX.X),
                             waits=[("pe", mm)])
                    p.op("scalar",
                         lambda sc: sc.activation(sqb[:, 0:2048], qh[hb],
                                                  AF.Square),
                         waits=[("pe", p.c("pe"))])
                    sq2 = p.c("ac")
                    for i in range(4):
                        p.op("tensor",
                             lambda t, i=i: t.matmul(
                                 pacc[2][0:1, :], onescol_b[:],
                                 sqb[:, i * 512:(i + 1) * 512],
                                 start=True, stop=True),
                             waits=[("ac", sq2), ("dv", p.c("dv"))], inc="pe")
                        mm = p.c("pe")
                        p.op("vector",
                             lambda v, i=i: v.reduce_max(mx8[:, 8 + i:9 + i],
                                                         pacc[2][0:1, :],
                                                         axis=AX.X),
                             waits=[("pe", mm)])
                    p.op("vector", lambda v: v.reduce_max(sc1[:, 0:1],
                                                          mx8[:, 0:nkc],
                                                          axis=AX.X))
                    p.op("vector", lambda v: v.reduce_max(sc1[:, 1:2],
                                                          mx8[:, 8:12],
                                                          axis=AX.X))
                    p.op("vector", lambda v: v.tensor_tensor(
                        sc1[:, 2:3], sc1[:, 0:1], sc1[:, 1:2], op=ALU.mult))
                    prodd = p.c("dv")
                    p.op("scalar", lambda sc: sc.activation(sc1[:, 3:4],
                                                            sc1[:, 2:3],
                                                            AF.Ln),
                         waits=[("dv", prodd)])
                    p.op("scalar", lambda sc: sc.activation(sc1[:, 3:4],
                                                            sc1[:, 3:4],
                                                            AF.Exp,
                                                            scale=0.5))
                    sqd = p.c("ac")
                    p.op("vector", lambda v: v.tensor_scalar_mul(
                        sc1[:, 3:4], sc1[:, 3:4], -SCALE), waits=[("ac", sqd)])
                    nbd = p.c("dv")
                    p.op("tensor",
                         lambda t, hb=hb: t.matmul(pacc[2][:, 0:1], ones1f[:],
                                                   sc1[0:1, 3:4],
                                                   start=True, stop=True),
                         waits=[("dv", nbd)], inc="pe")
                    nbp = p.c("pe")
                    p.op("vector",
                         lambda v, hb=hb: v.tensor_copy(negb[hb][:],
                                                        pacc[2][:, 0:1]),
                         waits=[("pe", nbp)])
                    negb_d[h] = p.c("dv")
                    for cq in range(E // 512):
                        def vtf(t, cq=cq, hb=hb):
                            for j in range(4):
                                kc = cq * 4 + j
                                inst = t.transpose(
                                    ptr[:, j * 128:(j + 1) * 128],
                                    vh[hb][:, kc * 128:(kc + 1) * 128],
                                    idb_s[:])
                            return inst
                        p.op("tensor", vtf,
                             waits=[("kv", ld), ("dv", p.c("dv"))], inc="pe")
                        vt = p.c("pe")
                        p.op("vector",
                             lambda v, cq=cq, hb=hb: v.tensor_copy(
                                 vtok[hb][:, cq * 512:(cq + 1) * 512],
                                 ptr[:, 0:512]),
                             waits=[("pe", vt)])
                    vt_d[h] = p.c("dv")

                loads(0, 0)
                loads(1, 0)
                bbound(0)
                scrb = [psc[0], psc[1], pacc[3]]
                cool_wall = []
                for h in range(H):
                    hb = h % 2
                    if 1 <= h and h + 1 < H:
                        loads(h + 1, av_end[h - 1])
                    if h >= 1 and COOL_N > 0 and h % COOL_EVERY == 0:
                        cw = p.dma("sync", "cl", xn_tok[0:1, 0:2048],
                                   kss[0, 0:1, 0:2048],
                                   waits=[("pe", av_end[h - 1])])
                        for _ in range(COOL_N - 1):
                            cw = p.dma("sync", "cl", xn_tok[0:1, 0:2048],
                                       kss[0, 0:1, 0:2048])
                        cool_wall = [("cl", cw)]
                    for qi, (slot, pos) in enumerate(prm["qtiles"]):
                        if h + 1 < H and qi == 1:
                            bbound(h + 1)
                        nch = 4 * (pos + 1)
                        exps = [None] * nch
                        avs = [None] * nch
                        ob = (h * 4 + qi) % 2
                        for step in range(nch + 2):
                          if step < nch:
                            kc = step
                            bi3 = kc % 3
                            bank = scrb[bi3]
                            sw = ([("ac", exps[kc - 3])] if kc >= 3
                                  else list(bank_guard.get(bi3, [])) +
                                       [("dv", negb_d[h]), ("dv", vt_d[h])])
                            if kc == 0:
                                sw = sw + cool_wall
                                cool_wall = []
                            p.op("tensor",
                                 lambda t, kc=kc, slot=slot, hb=hb, bank=bank:
                                 t.matmul(
                                     bank[:],
                                     kh[hb][:, kc * 128:(kc + 1) * 128],
                                     qh[hb][:, slot * 512:(slot + 1) * 512],
                                     start=True, stop=True),
                                 waits=sw, inc="pe")
                            smm = p.c("pe")
                            wexp = [("pe", smm)]
                            if kc >= nch - 4:
                                r = kc - (nch - 4)
                                p.op("vector",
                                     lambda v, bank=bank, r=r: v.tensor_tensor(
                                         bank[:], bank[:],
                                         mask_s[:, r * 512:(r + 1) * 512],
                                         op=ALU.add),
                                     waits=[("pe", smm), ("kv", mask_ld)])
                                wexp = [("dv", p.c("dv"))]
                            if kc >= 4:
                                wexp.append(("pe", avs[kc - 4]))
                            p.op("scalar",
                                 lambda sc, kc=kc, bank=bank, hb=hb:
                                 sc.activation(
                                     pb[kc % 4][:], bank[:],
                                     AF.Exp, scale=SCALE, bias=negb[hb][:]),
                                 waits=wexp)
                            exps[kc] = p.c("ac")

                          if step >= 2:
                            kc = step - 2

                            def avpair(t, kc=kc, nch=nch, ob=ob, hb=hb):
                                t.matmul(pacc[ob][:],
                                         vtok[hb][:, kc * 128:(kc + 1) * 128],
                                         pb[kc % 4][:],
                                         start=(kc == 0),
                                         stop=(kc == nch - 1))
                                return t.matmul(prow[:], onescol_b[:],
                                                pb[kc % 4][:],
                                                start=(kc == 0),
                                                stop=(kc == nch - 1))
                            aw = [("ac", exps[kc])]
                            if kc == 0:
                                aw += [("dv", out_guard.get(ob, 0)),
                                       ("dv", sum_guard)]
                            p.op("tensor", avpair, waits=aw, inc="pe")
                            avs[kc] = p.c("pe")
                        av_end[h] = avs[nch - 1]
                        p.op("vector", lambda v: v.reciprocal(rs_sb[:],
                                                              prow[:]),
                             waits=[("pe", av_end[h])])
                        sum_guard = rsd = p.c("dv")
                        p.op("tensor",
                             lambda t: t.matmul(psc[1][:], ones1f[:],
                                                rs_sb[:],
                                                start=True, stop=True),
                             waits=[("dv", rsd)], inc="pe")
                        rbd = p.c("pe")
                        p.op("vector", lambda v: v.tensor_copy(r_sb[:],
                                                               psc[1][:]),
                             waits=[("pe", rbd)])
                        rsbd = p.c("dv")
                        for bi3 in range(3):
                            last = max(kc for kc in range(nch)
                                       if kc % 3 == bi3)
                            bank_guard[bi3] = [("ac", exps[last])]
                        bank_guard[1] = bank_guard[1] + [("dv", rsbd)]
                        oi = slot % 2
                        p.op("vector",
                             lambda v, oi=oi, ob=ob: v.scalar_tensor_tensor(
                                 ost[oi][:], pacc[ob][:], 1.0, r_sb[:],
                                 op0=ALU.mult, op1=ALU.mult),
                             waits=[("dv", rsbd),
                                    ("ao", ost_store.get(oi, 0))])
                        out_guard[ob] = p.c("dv")
                        ostd = p.c("dv")
                        ost_store[oi] = p.dma(
                            "gpsimd", "ao",
                            ag_scr[h * 128:(h + 1) * 128,
                                   slot * 512:(slot + 1) * 512],
                            ost[oi][:], waits=[("dv", ostd)])
                    gpump(2)
                gpump(999)
                p.drain_arm()
            p.arms_end()
            if DEBUG_STOP == 1:
                p.op("sync", lambda e: e.nop(), waits=[("ao", p.c("ao"))])
                p.emit(block, gs_s.ap()[0:1, 0:1], gsel_ready)
                return nc

            p.scope = "gfin"
            p.op("tensor", lambda t: t.matmul(prow[:, 0:2], onescol_f[:],
                                              gacc[:, 0:2], start=True,
                                              stop=True),
                 waits=[("dv", p.c("dv")), ("ac", p.c("ac")),
                        ("pe", p.c("pe"))], inc="pe")
            gpe = p.c("pe")
            p.op("vector", lambda v: v.tensor_copy(gsc[:, 0:2], prow[:, 0:2]),
                 waits=[("pe", gpe)])
            p.op("vector", lambda v: v.tensor_scalar_mul(gsc[:, 0:2],
                                                         gsc[:, 0:2],
                                                         1.0 / (D * FF)))
            p.op("vector", lambda v: v.tensor_scalar_add(gsc[:, 2:4],
                                                         gsc[:, 0:2], EPS_B))
            p.op("vector", lambda v: v.tensor_scalar_mul(gsc[:, 4:6],
                                                         gsc[:, 2:4], 0.5))
            p.op("vector", lambda v: v.reciprocal(gsc[:, 2:4], gsc[:, 2:4]))
            for (dst, col) in ((gfc_b, 0), (gpj_b, 1), (gfcinv_b, 2),
                               (gpjinv_b, 3), (thr_fc, 4), (thr_pj, 5)):
                p.op("tensor",
                     lambda t, col=col: t.matmul(pcor[:, 0:1], ones1f[:],
                                                 gsc[0:1, col:col + 1],
                                                 start=True, stop=True),
                     waits=[("dv", p.c("dv"))], inc="pe")
                mm = p.c("pe")
                p.op("vector", lambda v, dst=dst: v.tensor_copy(dst[:],
                                                                pcor[:, 0:1]),
                     waits=[("pe", mm)])
            gbc = p.c("dv")

            p.scope = "p5"
            wo_mm = {}
            attn_free = {}
            for tq in range(4):
                tb = tq * 512
                wag = p.dma("sync", "ag", agsl[:],
                            ag_scr[:, tb:tb + 512]
                            .rearrange("(a p) t -> p a t", p=128),
                            waits=[("pe", p.c("pe")), ("dv", p.c("dv")),
                                   ("ac", p.c("ac"))])
                for tsub in range(4):
                    xw = p.dma("sync", "sx2", xrow[tsub % 2],
                               xown[tb + tsub * 128: tb + (tsub + 1) * 128, :],
                               waits=[("dv", p.c("dv"))])
                    p.op("vector",
                         lambda v, tsub=tsub: v.tensor_copy(h1t[tsub],
                                                            xrow[tsub % 2]),
                         waits=[("sx2", xw), ("h1s", p.c("h1s"))])
                h1i = p.c("dv")
                for ec in range(16):
                    eb = ec % 2
                    wwo = p.dma("gpsimd", "wo", wob[eb],
                                woT[:, ec * 128:(ec + 1) * 128]
                                .rearrange("(a p) e -> p a e", p=128),
                                waits=[("pe", wo_mm.get(eb, 0))])

                    def wmm(t, eb=eb):
                        for dc in range(16):
                            inst = t.matmul(pacc[eb][:], wob[eb][:, dc, :],
                                            agsl[:, dc, :],
                                            start=(dc == 0), stop=(dc == 15))
                        return inst
                    p.op("tensor", wmm,
                         waits=[("wo", wwo), ("ag", wag),
                                ("ac", attn_free.get(eb, 0))], inc="pe")
                    wod = p.c("pe")
                    wo_mm[eb] = wod
                    p.op("scalar",
                         lambda sc, eb=eb: sc.activation(attn_sb[eb],
                                                         pacc[eb][:],
                                                         AF.Copy),
                         waits=[("pe", wod)])
                    attnd = p.c("ac")
                    attn_free[eb] = attnd

                    def trf5(t, eb=eb):
                        for tsub in range(4):
                            inst = t.transpose(
                                ptrf[:, tsub * 128:(tsub + 1) * 128],
                                attn_sb[eb][:, tsub * 128:(tsub + 1) * 128],
                                idf_s[:])
                        return inst
                    p.op("tensor", trf5,
                         waits=[("ac", attnd), ("dv", p.c("dv"))], inc="pe")
                    trd = p.c("pe")
                    for tsub in range(4):
                        p.op("vector",
                             lambda v, tsub=tsub, ec=ec: v.tensor_tensor(
                                 h1t[tsub][:, ec * 128:(ec + 1) * 128],
                                 h1t[tsub][:, ec * 128:(ec + 1) * 128],
                                 ptrf[:, tsub * 128:(tsub + 1) * 128],
                                 op=ALU.add),
                             waits=[("pe", trd), ("dv", h1i)])
                h1d = p.c("dv")
                for tsub in range(4):
                    t128 = tq * 4 + tsub
                    p.dma("scalar", "h1s",
                          xout[tb + tsub * 128: tb + (tsub + 1) * 128, :],
                          h1t[tsub], waits=[("dv", h1d)])
                    for ci in range(4):
                        p.op("vector",
                             lambda v, tsub=tsub, ci=ci: v.bn_stats(
                                 stats[:, ci * 6:(ci + 1) * 6],
                                 h1t[tsub][:, ci * 512:(ci + 1) * 512]),
                             waits=[("dv", h1d)])
                    p.op("vector", lambda v: v.bn_aggr(mv[:], stats[:, 0:24]))
                    agg = p.c("dv")
                    p.op("scalar", lambda sc: sc.activation(rstd[:],
                                                            mv[:, 1:2],
                                                            AF.Ln,
                                                            bias=epsa_t[:]),
                         waits=[("dv", agg)])
                    p.op("scalar", lambda sc: sc.activation(rstd[:], rstd[:],
                                                            AF.Exp,
                                                            scale=-0.5))
                    rst = p.c("ac")
                    p.op("vector",
                         lambda v, tsub=tsub: v.tensor_scalar(
                             xn2, h1t[tsub], mv[:, 0:1], rstd[:],
                             op0=ALU.subtract, op1=ALU.mult),
                         waits=[("ac", rst)])
                    p.op("vector", lambda v: v.reduce_max(
                        gq[:], xn2, axis=AX.X, apply_absolute_value=True))
                    p.op("vector", lambda v: v.reciprocal(s1v[:], gq[:]))
                    p.op("vector", lambda v: v.tensor_scalar_mul(
                        s1v[:], s1v[:], 128.0))
                    p.op("vector", lambda v: v.tensor_scalar_mul(
                        s2v[:], gq[:], 1.0 / 128.0))
                    p.op("vector", lambda v: v.tensor_scalar(
                        clipt, xn2, s1v[:], 128.0, op0=ALU.mult, op1=ALU.min))
                    p.op("vector", lambda v: v.tensor_scalar(
                        xh_tok, clipt, -128.0, s2v[:], op0=ALU.max,
                        op1=ALU.mult))
                    for dq in range(4):
                        def trx(t, dq=dq):
                            for j in range(4):
                                dc = dq * 4 + j
                                inst = t.transpose(
                                    ptr[:, j * 128:(j + 1) * 128],
                                    xh_tok[:, dc * 128:(dc + 1) * 128],
                                    idb_s[:])
                            return inst
                        p.op("tensor", trx, waits=[("dv", p.c("dv"))],
                             inc="pe")
                        trp = p.c("pe")
                        p.op("vector",
                             lambda v, dq=dq, t128=t128: v.tensor_copy(
                                 xfm2[:, dq * 4:(dq + 1) * 4,
                                      t128 * 128:(t128 + 1) * 128],
                                 _r3(ptr[:, 0:512], 128)),
                             waits=[("pe", trp)])
            xfm2d = p.c("dv")
            h1_stores = p.c("h1s")

            p.scope = "p6"
            tern_wb = {}
            fcm_done = {}
            gel_last = 0
            hs_done = {}
            for oc in range(64):
                ob = oc % 2
                wb3 = oc % 3
                wst = "wf" if oc % 2 == 0 else "wf2"
                weng = "sync" if oc % 2 == 0 else "scalar"
                wld = p.dma(weng, wst, wfs[wb3][:],
                            wfcT[:, oc * 128:(oc + 1) * 128]
                            .rearrange("(a p) n -> p a n", p=128),
                            waits=[("dv", xfm2d if oc < 3
                                    else tern_wb.get(wb3, 0))])
                wf = wfs[wb3].rearrange("p a b -> p (a b)")
                p.op("vector",
                     lambda v, wf=wf: v.tensor_scalar(
                         zfc2.rearrange("p a b -> p (a b)"), wf, gfcinv_b[:],
                         -0.5, op0=ALU.mult, op1=ALU.is_le),
                     waits=[(wst, wld), ("dv", gbc),
                            ("pe", p.c("pe") if oc < 2
                             else fcm_done.get(ob, 0))])
                p.op("vector",
                     lambda v, wf=wf, ob=ob: v.scalar_tensor_tensor(
                         tfc[ob].rearrange("p a b -> p (a b)"),
                         wf, thr_fc[:],
                         zfc2.rearrange("p a b -> p (a b)"),
                         op0=ALU.is_ge, op1=ALU.subtract))
                tern_wb[wb3] = tern_done_oc = p.c("dv")

                fw = []
                if oc > 0 and oc % 8 == 0:
                    cw2v = p.dma("gpsimd", "cl2", xn_tok[0:1, 0:2048],
                                 kss[0, 0:1, 0:2048],
                                 waits=[("pe", fcm_done[1 - ob])])
                    cw2v = p.dma("gpsimd", "cl2", xn_tok[0:1, 0:2048],
                                 kss[0, 0:1, 0:2048])
                    fw = [("cl2", cw2v)]

                def fcm(t, ob=ob):
                    for dc in range(16):
                        for tq in range(4):
                            inst = t.matmul(pacc[tq][:], tfc[ob][:, dc, :],
                                            xfm2[:, dc,
                                                 tq * 512:(tq + 1) * 512],
                                            start=(dc == 0), stop=(dc == 15))
                    return inst
                p.op("tensor", fcm,
                     waits=[("dv", tern_done_oc), ("ac", gel_last)] + fw,
                     inc="pe")
                fcm_done[ob] = p.c("pe")
                for tq in range(4):
                    hb = tq % 2
                    p.op("scalar",
                         lambda sc, tq=tq, hb=hb: sc.activation(
                             hgel[hb][:], pacc[tq][:], AF.Gelu,
                             scale=gfc_b[:]),
                         waits=[("pe", fcm_done[ob]), ("dv", gbc),
                                ("hs", hs_done.get(hb, 0))])
                    gel_last = p.c("ac")
                    hs_done[hb] = p.dma(
                        "scalar", "hs",
                        h_scr[oc * 128:(oc + 1) * 128,
                              tq * 512:(tq + 1) * 512],
                        hgel[hb][:], waits=[("ac", gel_last)])
            h_stores = p.c("hs")

            p.scope = "p7a"
            ht_free = {0: 0, 1: 0}
            for ti in range(16):
                tb2 = ti % 2
                hbuf = htok if tb2 == 0 else htok2
                hst = "ht" if tb2 == 0 else "ht2"
                w = p.dma("sync", hst, hbuf,
                          h_scr[:, ti * 128:(ti + 1) * 128],
                          waits=[("hs", h_stores), ("dv", ht_free[tb2])] +
                                ([("pe", p.c("pe")), ("gp", p.c("gp"))]
                                 if ti < 2 else []),
                          transpose=True)
                for ci in range(16):
                    p.op("vector",
                         lambda v, ci=ci, hbuf=hbuf: v.bn_stats(
                             stats[:, ci * 6:(ci + 1) * 6],
                             hbuf[:, ci * 512:(ci + 1) * 512]),
                         waits=[(hst, w)])
                p.op("vector", lambda v: v.bn_aggr(mv[:], stats[:]))
                p.op("vector", lambda v, hbuf=hbuf: v.reduce_max(
                    maxh[:], hbuf, axis=AX.X))
                p.op("vector", lambda v, hbuf=hbuf: v.tensor_reduce(
                    minh[:], hbuf, axis=AX.X, op=ALU.min))
                ht_free[tb2] = p.c("dv")
                p.op("scalar", lambda sc: sc.activation(rstd[:], mv[:, 1:2],
                                                        AF.Ln,
                                                        bias=epsb_t[:]),
                     waits=[("dv", ht_free[tb2])])
                p.op("scalar", lambda sc: sc.activation(rstd[:], rstd[:],
                                                        AF.Exp, scale=-0.5))
                rst = p.c("ac")
                p.op("vector", lambda v: v.tensor_scalar(
                    ga[:], maxh[:], mv[:, 0:1], None, op0=ALU.subtract),
                     waits=[("ac", rst)])
                p.op("vector", lambda v: v.tensor_scalar(
                    gb[:], minh[:], mv[:, 0:1], -1.0, op0=ALU.subtract,
                    op1=ALU.mult))
                p.op("vector", lambda v: v.tensor_tensor(gq[:], ga[:], gb[:],
                                                         op=ALU.max))
                p.op("vector", lambda v: v.tensor_scalar(gq[:], gq[:],
                                                         rstd[:], None,
                                                         op0=ALU.mult))
                p.op("vector", lambda v: v.reciprocal(s1v[:], gq[:]))
                p.op("vector", lambda v: v.tensor_scalar_mul(s1v[:], s1v[:],
                                                             128.0))
                p.op("vector", lambda v: v.tensor_scalar_mul(s2v[:], gq[:],
                                                             1.0 / 128.0))
                p.op("vector", lambda v: v.tensor_scalar(s1v[:], s1v[:],
                                                         s2v[:], rstd[:],
                                                         op0=ALU.mult,
                                                         op1=ALU.mult))
                p.op("vector",
                     lambda v, ti=ti: v.tensor_scalar(
                         svec[:, ti:ti + 1], s1v[:], gpj_b[:], None,
                         op0=ALU.mult),
                     waits=[("dv", gbc)])
                p.op("vector",
                     lambda v, ti=ti: v.tensor_scalar(
                         msv[:], svec[:, ti:ti + 1], mv[:, 0:1], -1.0,
                         op0=ALU.mult, op1=ALU.mult))
                msd = p.c("dv")
                p.op("tensor", lambda t: t.matmul(prow[:, 0:128], msv[:],
                                                  idf_s[:],
                                                  start=True, stop=True),
                     waits=[("dv", msd)], inc="pe")
                msp = p.c("pe")
                p.op("vector",
                     lambda v, ti=ti: v.tensor_copy(
                         msrow[:, ti * 128:(ti + 1) * 128], prow[:, 0:128]),
                     waits=[("pe", msp)])
            p7a_d = p.c("dv")

            p.scope = "p7b"
            o2_done = {}
            o2_all = 0
            wp_done = {}
            for eb in range(4):
                ebc = eb * 512
                for half in range(8):
                    hb3 = half % 3
                    wst = "wp" if half % 2 == 0 else "wp2"
                    weng = "sync" if half % 2 == 0 else "scalar"
                    exw = []
                    if eb == 0 and half < 2:
                        exw = [("h1s", h1_stores)]
                    elif eb == 0 and half == 2:
                        exw = [("hs", h_stores)]
                    wld = p.dma(weng, wst, wps[hb3],
                                wpjT[half * 1024:(half + 1) * 1024,
                                     ebc:ebc + 512]
                                .rearrange("(a p) n -> p a n", p=128),
                                waits=[("dv", wp_done.get(hb3, 0))] + exw)
                    wfp = wps[hb3].rearrange("p a b -> p (a b)")
                    p.op("vector",
                         lambda v, wfp=wfp: v.tensor_scalar(
                             zpj2.rearrange("p a b -> p (a b)"), wfp,
                             gpjinv_b[:], -0.5, op0=ALU.mult, op1=ALU.is_le),
                         waits=[(wst, wld), ("dv", gbc),
                                ("pe", p.c("pe") if half < 3 else 0)])
                    p.op("vector",
                         lambda v, wfp=wfp, half=half:
                         v.scalar_tensor_tensor(
                             tpj[:, half * 8:(half + 1) * 8, :].rearrange(
                                 "p a b -> p (a b)"),
                             wfp, thr_pj[:],
                             zpj2.rearrange("p a b -> p (a b)"),
                             op0=ALU.is_ge, op1=ALU.subtract))
                    wp_done[hb3] = p.c("dv")
                ternd = p.c("dv")

                def csf(t):
                    for oc in range(64):
                        inst = t.matmul(prow[:], onescol_b[:], tpj[:, oc, :],
                                        start=(oc == 0), stop=(oc == 63))
                    return inst
                p.op("tensor", csf, waits=[("dv", ternd)], inc="pe")
                csd = p.c("pe")
                p.op("vector", lambda v: v.tensor_copy(crow[:], prow[:]),
                     waits=[("pe", csd)])
                crd = p.c("dv")

                for tq in range(4):
                    hwv = [None] * 64
                    pe_oc = [None] * 64
                    for pre in range(3):
                        hwv[pre] = p.dma(
                            "gpsimd", f"htl{pre}", htile[pre][:],
                            h_scr[pre * 128:(pre + 1) * 128,
                                  tq * 512:(tq + 1) * 512],
                            waits=[("hs", h_stores)])
                    pw = []
                    if (eb * 4 + tq) % 2 == 0 and eb + tq > 0:
                        cw2v = p.dma("gpsimd", "cl2", xn_tok[0:1, 0:2048],
                                     kss[0, 0:1, 0:2048],
                                     waits=[("pe", p.c("pe"))])
                        cw2v = p.dma("gpsimd", "cl2", xn_tok[0:1, 0:2048],
                                     kss[0, 0:1, 0:2048])
                        pw = [("cl2", cw2v)]
                    for oc in range(64):
                        def pjm(t, oc=oc):
                            for ts4 in range(4):
                                inst = t.matmul(
                                    pacc[ts4][:],
                                    htile[oc % 3][:,
                                                  ts4 * 128:(ts4 + 1) * 128],
                                    tpj[:, oc, :],
                                    start=(oc == 0), stop=(oc == 63))
                            return inst
                        p.op("tensor", pjm,
                             waits=[(f"htl{oc % 3}", hwv[oc]), ("dv", crd),
                                    ("dv", o2_all if oc == 0 else 0)] +
                                   (pw if oc == 0 else []),
                             inc="pe")
                        pe_oc[oc] = p.c("pe")
                        if oc + 3 < 64:
                            hwv[oc + 3] = p.dma(
                                "gpsimd", f"htl{oc % 3}", htile[oc % 3][:],
                                h_scr[(oc + 3) * 128:(oc + 4) * 128,
                                      tq * 512:(tq + 1) * 512],
                                waits=[("pe", pe_oc[oc])])
                    pjd = p.c("pe")
                    for ts4 in range(4):
                        t128 = tq * 4 + ts4
                        p.op("tensor",
                             lambda t, t128=t128: t.matmul(
                                 pcor[:],
                                 msrow[0:1, t128 * 128:(t128 + 1) * 128],
                                 crow[:], start=True, stop=True),
                             waits=[("dv", p.c("dv")), ("pe", pjd)], inc="pe")
                        cord = p.c("pe")
                        p.op("vector", lambda v: v.tensor_copy(csb[:],
                                                               pcor[:]),
                             waits=[("pe", cord)])
                        csbd = p.c("dv")
                        oi = ts4 % 2
                        p.op("vector",
                             lambda v, ts4=ts4, t128=t128, oi=oi:
                             v.scalar_tensor_tensor(
                                 o2sb[oi][:], pacc[ts4][:],
                                 svec[:, t128:t128 + 1], csb[:],
                                 op0=ALU.mult, op1=ALU.add),
                             waits=[("dv", csbd),
                                    ("o2s", o2_done.get(oi, 0)),
                                    ("pe", pjd)])
                        o2d = p.c("dv")
                        o2_done[oi] = p.dma(
                            "gpsimd", "o2s",
                            xout[t128 * 128:(t128 + 1) * 128, ebc:ebc + 512],
                            o2sb[oi][:],
                            waits=[("dv", o2d), ("h1s", h1_stores)],
                            accum_op=ALU.add)
                    o2_all = p.c("dv")

            p.op("sync", lambda e: e.nop(),
                 waits=[("o2s", p.c("o2s")), ("h1s", h1_stores),
                        ("hs", p.c("hs"))])
            if p.c("cl2") > 0:
                p.op("gpsimd", lambda e: e.nop(),
                     waits=[("cl2", p.c("cl2"))], _isdma=True)
            p.emit(block, gs_s.ap()[0:1, 0:1], gsel_ready)
    return nc



def _host_constants():
    inv = (np.float32(1.0) /
           np.float32(10000.0) ** (np.arange(0, HD, 2, dtype=np.float32) /
                                   np.float32(HD)))
    t = np.arange(S, dtype=np.float32)
    freqs = np.outer(t, inv).astype(np.float32)
    emb = np.concatenate([freqs, freqs], axis=1)
    cosf = np.ascontiguousarray(np.cos(emb).T.astype(np.float32))
    sinf = np.ascontiguousarray(np.sin(emb).T.astype(np.float32))
    rt = np.zeros((HD, HD), dtype=ml_dtypes.bfloat16)
    for i in range(64):
        rt[i + 64, i] = -1
        rt[i, i + 64] = 1
    idb = np.eye(128, dtype=ml_dtypes.bfloat16)
    idf = np.eye(128, dtype=np.float32)
    masks = np.zeros((128, 4 * 512), dtype=np.float32)
    pp = np.arange(128)[:, None]
    nn = np.arange(512)[None, :]
    for r in range(4):
        masks[:, r * 512:(r + 1) * 512] = np.where(r * 128 + pp <= nn, 0.0, NEG)
    return cosf, sinf, rt, idb, idf, masks


_NC_CACHE = {}
DEBUG_STOP = 0
COOL_N = 3
COOL_EVERY = 1


def kernel(L0_attn_x, L0_attn_Wq, L0_attn_Wk, L0_attn_Wv, L0_attn_Wo,
           L0_ffn_w_fc, L0_ffn_w_proj):
    x = np.ascontiguousarray(np.asarray(L0_attn_x, dtype=np.float32))
    Wq = np.asarray(L0_attn_Wq, dtype=np.float32)
    Wk = np.asarray(L0_attn_Wk, dtype=np.float32)
    Wv = np.asarray(L0_attn_Wv, dtype=np.float32)
    Wo = np.asarray(L0_attn_Wo, dtype=np.float32)
    wfc = np.asarray(L0_ffn_w_fc, dtype=np.float32)
    wpj = np.asarray(L0_ffn_w_proj, dtype=np.float32)
    cosf, sinf, rt, idb, idf, masks = _host_constants()

    if "nc" not in _NC_CACHE:
        _NC_CACHE["nc"] = build_fused()
    nc = _NC_CACHE["nc"]

    wqTh = np.ascontiguousarray(Wq.T)
    wkTh = np.ascontiguousarray(Wk.T)
    wvTh = np.ascontiguousarray(Wv.T)
    woTh = np.ascontiguousarray(Wo.T)
    wfcTh = np.ascontiguousarray(wfc.T)
    wpjTh = np.ascontiguousarray(wpj.T)
    in_maps = []
    for c in range(8):
        b, g = c // 2, c % 2
        if g == 0:
            xo = np.concatenate([x[b, 0:1024], x[b, 3072:4096]], axis=0)
        else:
            xo = x[b, 1024:3072]
        in_maps.append({
            "xb": x[b],
            "xown": np.ascontiguousarray(xo),
            "wqT": wqTh, "wkT": wkTh, "wvT": wvTh, "woT": woTh,
            "wfcT": wfcTh, "wpjT": wpjTh,
            "cosf": cosf, "sinf": sinf, "rtc": rt, "idbi": idb, "idfi": idf,
            "masks": masks,
            "gseli": np.array([[g]], dtype=np.int32),
        })
    res = run_bass_kernel_spmd(nc, in_maps, core_ids=list(range(8)))
    out = np.empty((B, S, D), dtype=np.float32)
    for c in range(8):
        b, g = c // 2, c % 2
        xo = res.results[c]["xout"]
        if g == 0:
            out[b, 0:1024] = xo[0:1024]
            out[b, 3072:4096] = xo[1024:2048]
        else:
            out[b, 1024:3072] = xo
    return out


# version 11
# speedup vs baseline: 1.0421x; 1.0421x over previous
import math
from contextlib import ExitStack

import numpy as np
import ml_dtypes
import concourse.bass as bass
import concourse.mybir as mybir
from concourse.bass_utils import run_bass_kernel_spmd

F32 = mybir.dt.float32
BF16 = mybir.dt.bfloat16
I32 = mybir.dt.int32
AF = mybir.ActivationFunctionType
ALU = mybir.AluOpType
AX = mybir.AxisListType

B, S, D, H, HD, FF = 4, 4096, 2048, 16, 128, 8192
TOWN = 2048
SCALE = 1.0 / math.sqrt(HD)
EPS_A = 1e-5
EPS_B = 1e-6
NEG = -1.0e9

ENGS = ("sync", "gpsimd", "vector", "scalar", "tensor")
SELF_SYNC = {"vector": "dv", "scalar": "ac", "gpsimd": "gp"}
COMPUTE_SEMS = ("dv", "ac", "pe", "gp")
SEM_OWNER = {"dv": "vector", "ac": "scalar", "pe": "tensor", "gp": "gpsimd"}

ARM_PARAMS = [
    dict(extent=4096,
         sts=[(0, 16, 4, [(0, 0), (1, 1)]), (2048, 16, 4, [(2, 2), (3, 3)])],
         qtiles=[(0, 0), (1, 1), (2, 6), (3, 7)]),
    dict(extent=3072,
         sts=[(0, 16, 4, [(2, 0), (3, 1)]), (2048, 8, 2, [(0, 2), (1, 3)])],
         qtiles=[(0, 2), (1, 3), (2, 4), (3, 5)]),
]


class Prog:

    def __init__(self, nc, ctx):
        self.nc = nc
        self.ctx = ctx
        self.segments = [("common", {e: [] for e in ENGS})]
        self.cnt = {}
        self.sems = {}
        self.stream_eng = {}
        self.arm_state = None
        self.arm_pads = None
        self.scope = "init"

    def mksem(self, name):
        self.sems[name] = self.ctx.enter_context(self.nc.semaphore(name))
        self.cnt[name] = 0

    def mksems(self, names):
        for name in names:
            self.mksem(name)

    def _cur(self):
        seg = self.segments[-1]
        return seg[1] if seg[0] == "common" else seg[1][-1]

    def arms_begin(self):
        assert self.arm_state is None
        self.segments.append(("arms", []))
        self.arm_state = (dict(self.cnt), [])

    def arm(self):
        snap, ends = self.arm_state
        if self.segments[-1][1]:
            ends.append(dict(self.cnt))
        self.cnt = dict(snap)
        self.segments[-1][1].append({e: [] for e in ENGS})

    def drain_arm(self):
        snap, _ = self.arm_state
        for st, eng in list(self.stream_eng.items()):
            v = self.cnt.get(st, 0)
            if v > snap.get(st, 0):
                self.op(eng, lambda e: e.nop(), waits=[(st, v)], _isdma=True)

    def arms_end(self):
        snap, ends = self.arm_state
        ends.append(dict(self.cnt))
        final = {k: max(e.get(k, 0) for e in ends) for k in self.cnt}
        self.arm_pads = [{k: final[k] - e.get(k, 0) for k in final
                          if k in COMPUTE_SEMS} for e in ends]
        self.arm_ends = ends
        self.cnt = final
        self.arm_state = None
        self.segments.append(("common", {e: [] for e in ENGS}))

    def op(self, eng, fn, waits=(), inc=None, incn=1, _isdma=False):
        waits = list(waits)
        if not _isdma and eng in SELF_SYNC:
            own = SELF_SYNC[eng]
            if inc is None:
                inc, incn = own, 1
            assert inc == own, (eng, inc)
            waits.append((own, self.cnt[own]))
        self._cur()[eng].append((tuple(waits), fn, inc, incn, self.scope))
        if inc is not None:
            self.cnt[inc] = self.cnt.get(inc, 0) + incn
            return self.cnt[inc]
        return None

    def dma(self, eng, stream, out, in_, waits=(), **kw):
        if stream not in self.sems:
            self.mksem(stream)
        assert self.stream_eng.setdefault(stream, eng) == eng, stream
        waits = list(waits) + [(stream, self.cnt.get(stream, 0))]
        return self.op(eng, lambda e: e.dma_start(out=out, in_=in_, **kw),
                       waits=waits, inc=stream, incn=16, _isdma=True)

    def c(self, sem):
        return self.cnt.get(sem, 0)

    def emit(self, block, gsel_ap, gsel_ready):
        sems = self.sems

        cur_scope = [None, 0]

        def emit_ops(eng, ops, hw):
            for waits, fn, inc, incn, scope in ops:
                if scope != cur_scope[0]:
                    if cur_scope[0] is not None:
                        self.nc.leave_named_scope(cur_scope[0], cur_scope[1], False)
                    sid, _ = self.nc.enter_named_scope(scope, False)
                    cur_scope[0], cur_scope[1] = scope, sid
                for (s, v) in waits:
                    if v > 0 and hw.get(s, 0) < v:
                        eng.wait_ge(sems[s], v)
                        hw[s] = v
                inst = fn(eng)
                if inc is not None:
                    inst.then_inc(sems[inc], incn)

        for eng_name in ENGS:
            def body(eng, eng_name=eng_name):
                hw = {}
                post_arms = False
                for seg in self.segments:
                    if seg[0] == "common":
                        if post_arms:
                            eng.wait_ge(sems["join"], 5)
                            post_arms = False
                        emit_ops(eng, seg[1][eng_name], hw)
                        continue
                    if hw.get("gsel_ld", 0) < gsel_ready:
                        eng.wait_ge(sems["gsel_ld"], gsel_ready)
                        hw["gsel_ld"] = gsel_ready
                    with eng.register(f"gsel_{eng_name}") as gr:
                        eng.reg_load(gr, gsel_ap)
                        for i in eng.Switch(eng.snap(gr, min_val=0, max_val=1), 2):
                            emit_ops(eng, seg[1][i][eng_name], dict(hw))
                            for sem, pad in self.arm_pads[i].items():
                                if pad > 0 and SEM_OWNER[sem] == eng_name:
                                    eng.wait_ge(sems[sem],
                                                self.arm_ends[i][sem])
                                    eng.sem_inc(sems[sem], pad)
                            eng.sem_inc(sems["join"], 1)
                    hw = {}
                    post_arms = True
                if cur_scope[0] is not None:
                    self.nc.leave_named_scope(cur_scope[0], cur_scope[1], False)
                    cur_scope[0] = None
            getattr(block, eng_name)(body)


def _r3(ap2d, b):
    return ap2d.rearrange("p (a b) -> p a b", b=b)


def build_fused():
    nc = bass.Bass("TRN2", target_bir_lowering=False)
    ap = lambda t: t.ap()
    xb = ap(nc.dram_tensor("xb", [S, D], F32, kind="ExternalInput"))
    xown = ap(nc.dram_tensor("xown", [TOWN, D], F32, kind="ExternalInput"))
    wqT = ap(nc.dram_tensor("wqT", [D, D], F32, kind="ExternalInput"))
    wkT = ap(nc.dram_tensor("wkT", [D, D], F32, kind="ExternalInput"))
    wvT = ap(nc.dram_tensor("wvT", [D, D], F32, kind="ExternalInput"))
    woT = ap(nc.dram_tensor("woT", [D, D], F32, kind="ExternalInput"))
    wfcT = ap(nc.dram_tensor("wfcT", [D, FF], F32, kind="ExternalInput"))
    wpjT = ap(nc.dram_tensor("wpjT", [FF, D], F32, kind="ExternalInput"))
    cosf = ap(nc.dram_tensor("cosf", [HD, S], F32, kind="ExternalInput"))
    sinf = ap(nc.dram_tensor("sinf", [HD, S], F32, kind="ExternalInput"))
    rtc = ap(nc.dram_tensor("rtc", [HD, HD], BF16, kind="ExternalInput"))
    idbi = ap(nc.dram_tensor("idbi", [128, 128], BF16, kind="ExternalInput"))
    idfi = ap(nc.dram_tensor("idfi", [128, 128], F32, kind="ExternalInput"))
    masks = ap(nc.dram_tensor("masks", [128, 4 * 512], F32, kind="ExternalInput"))
    gseli = ap(nc.dram_tensor("gseli", [1, 1], I32, kind="ExternalInput"))

    xout = ap(nc.dram_tensor("xout", [TOWN, D], F32, kind="ExternalOutput"))

    qss = ap(nc.dram_tensor("qss", [H, HD, TOWN], BF16))
    kss = ap(nc.dram_tensor("kss", [H, HD, S], BF16))
    vss = ap(nc.dram_tensor("vss", [H, HD, S], BF16))
    ag_scr = ap(nc.dram_tensor("ag_scr", [D, TOWN], BF16))
    h_scr = ap(nc.dram_tensor("h_scr", [FF, TOWN], BF16))

    with ExitStack() as ctx:
        sb = lambda name, shape, dt: ctx.enter_context(nc.sbuf_tensor(name, shape, dt))
        psa = lambda name, shape, dt: ctx.enter_context(nc.psum_tensor(name, shape, dt))

        rt_s = sb("rt_s", [HD, HD], BF16)
        idb_s = sb("idb_s", [128, 128], BF16)
        idf_s = sb("idf_s", [128, 128], F32)
        gs_s = sb("gs_s", [1, 1], I32)
        ones1f = sb("ones1f", [1, 128], F32)
        onescol_f = sb("onescol_f", [128, 1], F32)
        onescol_b = sb("onescol_b", [128, 1], BF16)
        epsa_t = sb("epsa_t", [128, 1], F32)
        epsb_t = sb("epsb_t", [128, 1], F32)
        gdump = sb("gdump", [128, 1024], F32)
        gtmp = sb("gtmp", [128, 1], F32)
        gacc = sb("gacc", [128, 2], F32)
        gsc = sb("gsc", [1, 8], F32)
        gfc_b = sb("gfc_b", [128, 1], F32)
        gpj_b = sb("gpj_b", [128, 1], F32)
        gfcinv_b = sb("gfcinv_b", [128, 1], F32)
        gpjinv_b = sb("gpjinv_b", [128, 1], F32)
        thr_fc = sb("thr_fc", [128, 1], F32)
        thr_pj = sb("thr_pj", [128, 1], F32)
        stats = sb("stats", [128, 96], F32)
        mv = sb("mv", [128, 2], F32)
        rstd = sb("rstd", [128, 1], F32)
        xn_tok = sb("xn_tok", [128, D], BF16)
        raw = [sb(f"raw{i}", [128, 512], BF16) for i in range(2)]
        t1b = [sb("t1b0", [128, 512], F32)] * 2
        t2b = [sb("t2b0", [128, 512], F32)] * 2
        stg = [sb(f"stg{i}", [128, 512], BF16) for i in range(4)]
        mx8 = sb("mx8", [1, 16], F32)
        sc1 = sb("sc1", [1, 4], F32)
        negb = [sb(f"negb{i}", [128, 1], F32) for i in range(2)]
        rs_sb = sb("rs_sb", [1, 512], F32)
        r_sb = sb("r_sb", [128, 512], F32)
        ost = [sb(f"ost{i}", [128, 512], BF16) for i in range(2)]
        gq = sb("gq", [128, 1], F32)
        s1v = sb("s1v", [128, 1], F32)
        s2v = sb("s2v", [128, 1], F32)
        hgel = [sb(f"hgel{i}", [128, 512], BF16) for i in range(2)]
        maxh = sb("maxh", [128, 1], F32)
        minh = sb("minh", [128, 1], F32)
        ga = sb("ga", [128, 1], F32)
        gb = sb("gb", [128, 1], F32)
        svec = sb("svec", [128, 16], F32)
        msv = sb("msv", [128, 1], F32)
        msrow = sb("msrow", [1, TOWN], F32)
        crow = sb("crow", [1, 512], F32)
        htile = [sb(f"htile{i}", [128, 512], BF16) for i in range(3)]
        csb = sb("csb", [128, 512], F32)
        o2sb = [sb(f"o2sb{i}", [128, 512], F32) for i in range(2)]

        arena = sb("arena", [128, 49152], BF16)
        ar = arena.ap()
        xn_fm = _r3(ar[:, 0:32768], 2048)
        wbf = [_r3(ar[:, 32768 + i * 8192: 32768 + (i + 1) * 8192], 512)
               for i in range(2)]
        kh = [ar[:, 0:4096], ar[:, 4096:8192]]
        vh = [ar[:, 8192:12288], ar[:, 12288:16384]]
        qh = [ar[:, 16384:18432], ar[:, 18432:20480]]
        vtok = [ar[:, 20480:24576], ar[:, 30720:34816]]
        pb = [ar[:, 24576 + i * 512: 24576 + (i + 1) * 512] for i in range(4)]
        sqb = ar[:, 26624:30720]
        mask_s = ar[:, 45056:49152].bitcast(F32)
        xfm2 = _r3(ar[:, 0:32768], 2048)
        agsl = _r3(ar[:, 32768:40960], 512)
        wob = [_r3(ar[:, 40960 + i * 2048: 40960 + (i + 1) * 2048], 128)
               for i in range(2)]
        tfc = [_r3(ar[:, 45056 + i * 2048: 45056 + (i + 1) * 2048], 128)
               for i in range(2)]
        xh_tok = ar[:, 45056:47104]
        attn_sb = [ar[:, 47104 + i * 1024: 47104 + (i + 1) * 1024].bitcast(F32)
                   for i in range(2)]
        htok = ar[:, 32768:40960]
        htok2 = ar[:, 40960:49152]
        tpj = _r3(ar[:, 0:32768], 512)

        arfa = sb("arfa", [128, 8192], F32)
        cos_s = arfa.ap()[:, 0:2048]
        sin_s = arfa.ap()[:, 2048:4096]
        xt = [arfa.ap()[:, 4096:6144], arfa.ap()[:, 6144:8192]]
        h1t = [arfa.ap()[:, i * 2048:(i + 1) * 2048] for i in range(4)]

        arfb = sb("arfb", [128, 8192], F32)
        xn2 = arfb.ap()[:, 0:2048]
        clipt = arfb.ap()[:, 2048:4096]
        xrow = [arfb.ap()[:, 4096 + i * 2048: 4096 + (i + 1) * 2048]
                for i in range(2)]
        wfs = [_r3(arfb.ap()[:, i * 2048:(i + 1) * 2048], 128)
               for i in range(3)]
        zfc2 = _r3(arfb.ap()[:, 6144:8192], 128)
        zpj2 = _r3(arfb.ap()[:, 4096:8192], 512)
        gdump2 = arfb.ap()[:, 0:1024]
        wps = [_r3(arfa.ap()[:, 0:4096], 512),
               _r3(arfa.ap()[:, 4096:8192], 512),
               _r3(arfb.ap()[:, 0:4096], 512)]
        grow_t = arfb.ap()[0:1, 0:1024]
        grow_a = [arfb.ap()[0:1, 1024 + i * 1024: 1024 + (i + 1) * 1024]
                  for i in range(2)]

        pacc = [psa(f"pacc{i}", [128, 512], F32) for i in range(4)]
        ptr = psa("ptr", [128, 512], BF16)
        psc = [psa(f"psc{i}", [128, 512], F32) for i in range(2)]
        prow = psa("prow", [1, 512], F32)
        ptrf = psc[0]
        pcor = psc[1]

        p = Prog(nc, ctx)
        p.mksems(("dv", "ac", "pe", "gp", "join"))

        gst = {}

        def gpump_reset():
            gst["i"] = 0
            gst["add"] = p.c("dv")
            gst["abs"] = p.c("dv")
            gst["abs0"] = gst["abs1"] = p.c("dv")

        def gpump(n):
            for _ in range(n):
                i = gst["i"]
                if i >= 256:
                    return
                gst["i"] += 1
                gi, si = (0, i) if i < 128 else (1, i - 128)
                src = (wfcT if gi == 0 else wpjT).rearrange("a b -> (a b)")
                blk = src[si * 131072:(si + 1) * 131072].rearrange(
                    "(q n) -> q n", n=1024)
                gpar = i % 2
                gbuf = gdump[:] if gpar == 0 else gdump2
                gstn = "gm" if gpar == 0 else "gm2"
                w = p.dma("scalar", gstn, gbuf, blk,
                          waits=[("dv", gst["abs%d" % gpar])])
                gst["abs%d" % gpar] = gst["abs"] = p.op(
                    "vector",
                    lambda v, gbuf=gbuf: v.tensor_reduce(
                        gtmp[:], gbuf, axis=AX.X, op=ALU.add,
                        apply_absolute_value=True),
                    waits=[(gstn, w)])
                gst["add"] = p.op(
                    "vector",
                    lambda v, gi=gi: v.tensor_tensor(
                        gacc[:, gi:gi + 1], gacc[:, gi:gi + 1], gtmp[:],
                        op=ALU.add))

        with nc.Block() as block:
            p.scope = "c0"
            p.dma("sync", "gsel_ld", gs_s[:], gseli)
            gsel_ready = 16
            p.dma("sync", "consts", rt_s[:], rtc)
            p.dma("sync", "consts", idb_s[:], idbi)
            p.dma("sync", "consts", idf_s[:], idfi)
            consts = p.c("consts")
            p.op("vector", lambda v: v.memset(ones1f[:], 1.0))
            p.op("vector", lambda v: v.memset(onescol_f[:], 1.0))
            p.op("vector", lambda v: v.memset(onescol_b[:], 1.0))
            p.op("vector", lambda v: v.memset(epsa_t[:], EPS_A))
            p.op("vector", lambda v: v.memset(epsb_t[:], EPS_B))
            p.op("vector", lambda v: v.memset(gacc[:], 0.0))
            memsets = p.c("dv")
            gst["ms"] = memsets

            p.arms_begin()
            for g in range(2):
                p.arm()
                prm = ARM_PARAMS[g]
                gpump_reset()
                wr_done = {}
                xn_last = {}
                bank_free = {}
                ost_store = {}

                p.scope = "p2"
                for (t0, n_xt, n_kv_tq, own) in prm["sts"]:
                    p.dma("sync", "cs", cos_s, cosf[:, t0:t0 + 2048],
                          waits=[("gp", p.c("gp")), ("dv", p.c("dv"))])
                    csw = p.dma("sync", "cs", sin_s, sinf[:, t0:t0 + 2048])
                    for ti in range(n_xt):
                        buf = xt[ti % 2]
                        w = p.dma("sync", "sx", buf,
                                  xb[t0 + ti * 128: t0 + (ti + 1) * 128, :],
                                  waits=[("dv", xn_last.get(ti % 2, 0))])
                        for ci in range(4):
                            p.op("vector",
                                 lambda v, buf=buf, ci=ci: v.bn_stats(
                                     stats[:, ci * 6:(ci + 1) * 6],
                                     buf[:, ci * 512:(ci + 1) * 512]),
                                 waits=[("sx", w), ("dv", memsets)])
                        p.op("vector", lambda v: v.bn_aggr(mv[:], stats[:, 0:24]))
                        agg = p.c("dv")
                        p.op("scalar",
                             lambda sc: sc.activation(rstd[:], mv[:, 1:2],
                                                      AF.Ln, bias=epsa_t[:]),
                             waits=[("dv", agg)])
                        p.op("scalar",
                             lambda sc: sc.activation(rstd[:], rstd[:], AF.Exp,
                                                      scale=-0.5))
                        rst = p.c("ac")
                        p.op("vector",
                             lambda v, buf=buf: v.tensor_scalar(
                                 xn_tok[:], buf, mv[:, 0:1], rstd[:],
                                 op0=ALU.subtract, op1=ALU.mult),
                             waits=[("ac", rst)])
                        xn_last[ti % 2] = p.c("dv")
                        for dq in range(4):
                            def trf(t, dq=dq):
                                for j in range(4):
                                    dc = dq * 4 + j
                                    inst = t.transpose(
                                        ptr[:, j * 128:(j + 1) * 128],
                                        xn_tok[:, dc * 128:(dc + 1) * 128],
                                        idb_s[:])
                                return inst
                            p.op("tensor", trf,
                                 waits=[("dv", p.c("dv")), ("consts", consts)],
                                 inc="pe")
                            trp = p.c("pe")
                            p.op("vector",
                                 lambda v, dq=dq, ti=ti: v.tensor_copy(
                                     xn_fm[:, dq * 4:(dq + 1) * 4,
                                           ti * 128:(ti + 1) * 128],
                                     _r3(ptr[:, 0:512], 128)),
                                 waits=[("pe", trp)])
                        gpump(2)
                    xnfm_done = p.c("dv")
                    mm_q = {}
                    for (wsrc, scr, dorope, tqs) in (
                            (wkT, kss, True,
                             [(tq, tq) for tq in range(n_kv_tq)]),
                            (wvT, vss, False,
                             [(tq, tq) for tq in range(n_kv_tq)]),
                            (wqT, qss, True, own)):
                        for qtr in range(4):
                            wb = qtr % 2
                            wl = p.dma("gpsimd", "wl", wbf[wb],
                                       wsrc[:, qtr * 512:(qtr + 1) * 512]
                                       .rearrange("(a p) e -> p a e", p=128),
                                       waits=[("pe", mm_q.get(wb, 0))])
                            for ec4 in range(4):
                                h = qtr * 4 + ec4
                                for (tq, slot) in tqs:
                                    bidx = (ec4 * 4 + tq) % 4
                                    bank = pacc[bidx]

                                    def mmf(t, ec4=ec4, tq=tq, bank=bank,
                                            wb=wb):
                                        for dc in range(16):
                                            inst = t.matmul(
                                                bank[:],
                                                wbf[wb][:, dc,
                                                        ec4 * 128:
                                                        (ec4 + 1) * 128],
                                                xn_fm[:, dc,
                                                      tq * 512:(tq + 1) * 512],
                                                start=(dc == 0),
                                                stop=(dc == 15))
                                        return inst
                                    p.op("tensor", mmf,
                                         waits=[("wl", wl), ("dv", xnfm_done),
                                                ("dv", bank_free.get(bidx, 0))],
                                         inc="pe")
                                    mmn = p.c("pe")
                                    mm_q[wb] = mmn
                                    si = (h * 4 + tq) % 2
                                    so = (h * 4 + tq) % 4
                                    if dorope:
                                        p.op("vector",
                                             lambda v, bank=bank, si=si:
                                             v.tensor_copy(raw[si][:], bank[:]),
                                             waits=[("pe", mmn)])
                                        rawn = p.c("dv")
                                        bank_free[bidx] = rawn
                                        p.op("tensor",
                                             lambda t, si=si: t.matmul(
                                                 psc[1][:], rt_s[:],
                                                 raw[si][:],
                                                 start=True, stop=True),
                                             waits=[("dv", rawn)], inc="pe")
                                        rotn = p.c("pe")
                                        p.op("gpsimd",
                                             lambda gg, si=si, tq=tq:
                                             gg.tensor_tensor(
                                                 t1b[si][:], raw[si][:],
                                                 cos_s[:, tq * 512:
                                                       (tq + 1) * 512],
                                                 op=ALU.mult),
                                             waits=[("dv", rawn), ("cs", csw)])
                                        gpn = p.c("gp")
                                        p.op("vector",
                                             lambda v, si=si, tq=tq:
                                             v.tensor_tensor(
                                                 t2b[si][:], psc[1][:],
                                                 sin_s[:, tq * 512:
                                                       (tq + 1) * 512],
                                                 op=ALU.mult),
                                             waits=[("pe", rotn), ("cs", csw)])
                                        p.op("vector",
                                             lambda v, si=si, so=so:
                                             v.tensor_tensor(
                                                 stg[so][:], t2b[si][:],
                                                 t1b[si][:], op=ALU.add),
                                             waits=[("gp", gpn),
                                                    ("wr", wr_done.get(so, 0))])
                                    else:
                                        p.op("vector",
                                             lambda v, bank=bank, so=so:
                                             v.tensor_copy(stg[so][:],
                                                           bank[:]),
                                             waits=[("pe", mmn),
                                                    ("wr", wr_done.get(so, 0))])
                                        bank_free[bidx] = p.c("dv")
                                    stn = p.c("dv")
                                    dst = (scr[h, :,
                                               slot * 512:(slot + 1) * 512]
                                           if wsrc is wqT else
                                           scr[h, :, t0 + tq * 512:
                                               t0 + (tq + 1) * 512])
                                    wr_done[so] = p.dma(
                                        "gpsimd", "wr", dst, stg[so][:],
                                        waits=[("dv", stn)])
                                gpump(1)

                wr_all = p.c("wr")
                E = prm["extent"]
                nkc = E // 512

                p.scope = "p3"
                mask_ld = p.dma("sync", "kv", mask_s, masks,
                                waits=[("pe", p.c("pe")), ("dv", p.c("dv")),
                                       ("gp", p.c("gp")), ("ac", p.c("ac"))])
                av_end = {}
                ld_h = {}
                negb_d = {}
                vt_d = {}
                out_guard = {}
                sum_guard = 0
                bank_guard = {}
                def loads(h, wpe):
                    hb = h % 2
                    p.dma("sync", "kv", kh[hb][:, 0:E], kss[h][:, 0:E],
                          waits=[("wr", wr_all), ("pe", wpe)])
                    p.dma("sync", "kv", vh[hb][:, 0:E], vss[h][:, 0:E])
                    ld_h[h] = p.dma("sync", "kv", qh[hb], qss[h])

                def bbound(h):
                    hb = h % 2
                    ld = ld_h[h]
                    p.op("scalar",
                         lambda sc: sc.activation(sqb[:, 0:E], kh[hb][:, 0:E],
                                                  AF.Square),
                         waits=[("kv", ld), ("pe", av_end.get(h - 1, 0))])
                    sq1 = p.c("ac")
                    for i in range(nkc):
                        p.op("tensor",
                             lambda t, i=i: t.matmul(
                                 pacc[2][0:1, :], onescol_b[:],
                                 sqb[:, i * 512:(i + 1) * 512],
                                 start=True, stop=True),
                             waits=[("ac", sq1), ("dv", p.c("dv"))], inc="pe")
                        mm = p.c("pe")
                        p.op("vector",
                             lambda v, i=i: v.reduce_max(mx8[:, i:i + 1],
                                                         pacc[2][0:1, :],
                                                         axis=AX.X),
                             waits=[("pe", mm)])
                    p.op("scalar",
                         lambda sc: sc.activation(sqb[:, 0:2048], qh[hb],
                                                  AF.Square),
                         waits=[("pe", p.c("pe"))])
                    sq2 = p.c("ac")
                    for i in range(4):
                        p.op("tensor",
                             lambda t, i=i: t.matmul(
                                 pacc[2][0:1, :], onescol_b[:],
                                 sqb[:, i * 512:(i + 1) * 512],
                                 start=True, stop=True),
                             waits=[("ac", sq2), ("dv", p.c("dv"))], inc="pe")
                        mm = p.c("pe")
                        p.op("vector",
                             lambda v, i=i: v.reduce_max(mx8[:, 8 + i:9 + i],
                                                         pacc[2][0:1, :],
                                                         axis=AX.X),
                             waits=[("pe", mm)])
                    p.op("vector", lambda v: v.reduce_max(sc1[:, 0:1],
                                                          mx8[:, 0:nkc],
                                                          axis=AX.X))
                    p.op("vector", lambda v: v.reduce_max(sc1[:, 1:2],
                                                          mx8[:, 8:12],
                                                          axis=AX.X))
                    p.op("vector", lambda v: v.tensor_tensor(
                        sc1[:, 2:3], sc1[:, 0:1], sc1[:, 1:2], op=ALU.mult))
                    prodd = p.c("dv")
                    p.op("scalar", lambda sc: sc.activation(sc1[:, 3:4],
                                                            sc1[:, 2:3],
                                                            AF.Ln),
                         waits=[("dv", prodd)])
                    p.op("scalar", lambda sc: sc.activation(sc1[:, 3:4],
                                                            sc1[:, 3:4],
                                                            AF.Exp,
                                                            scale=0.5))
                    sqd = p.c("ac")
                    p.op("vector", lambda v: v.tensor_scalar_mul(
                        sc1[:, 3:4], sc1[:, 3:4], -SCALE), waits=[("ac", sqd)])
                    nbd = p.c("dv")
                    p.op("tensor",
                         lambda t, hb=hb: t.matmul(pacc[2][:, 0:1], ones1f[:],
                                                   sc1[0:1, 3:4],
                                                   start=True, stop=True),
                         waits=[("dv", nbd)], inc="pe")
                    nbp = p.c("pe")
                    p.op("vector",
                         lambda v, hb=hb: v.tensor_copy(negb[hb][:],
                                                        pacc[2][:, 0:1]),
                         waits=[("pe", nbp)])
                    negb_d[h] = p.c("dv")
                    for cq in range(E // 512):
                        def vtf(t, cq=cq, hb=hb):
                            for j in range(4):
                                kc = cq * 4 + j
                                inst = t.transpose(
                                    ptr[:, j * 128:(j + 1) * 128],
                                    vh[hb][:, kc * 128:(kc + 1) * 128],
                                    idb_s[:])
                            return inst
                        p.op("tensor", vtf,
                             waits=[("kv", ld), ("dv", p.c("dv"))], inc="pe")
                        vt = p.c("pe")
                        p.op("vector",
                             lambda v, cq=cq, hb=hb: v.tensor_copy(
                                 vtok[hb][:, cq * 512:(cq + 1) * 512],
                                 ptr[:, 0:512]),
                             waits=[("pe", vt)])
                    vt_d[h] = p.c("dv")

                loads(0, 0)
                loads(1, 0)
                bbound(0)
                scrb = [psc[0], psc[1], pacc[3]]
                cool_wall = []
                for h in range(H):
                    hb = h % 2
                    if 1 <= h and h + 1 < H:
                        loads(h + 1, av_end[h - 1])
                    if h >= 1 and COOL_N > 0 and h % COOL_EVERY == 0:
                        cw = p.dma("sync", "cl", xn_tok[0:1, 0:2048],
                                   kss[0, 0:1, 0:2048],
                                   waits=[("pe", av_end[h - 1])])
                        for _ in range(COOL_N - 1):
                            cw = p.dma("sync", "cl", xn_tok[0:1, 0:2048],
                                       kss[0, 0:1, 0:2048])
                        cool_wall = [("cl", cw)]
                    for qi, (slot, pos) in enumerate(prm["qtiles"]):
                        if h + 1 < H and qi == 1:
                            bbound(h + 1)
                        nch = 4 * (pos + 1)
                        exps = [None] * nch
                        avs = [None] * nch
                        ob = (h * 4 + qi) % 2
                        for step in range(nch + 2):
                          if step < nch:
                            kc = step
                            bi3 = kc % 3
                            bank = scrb[bi3]
                            sw = ([("ac", exps[kc - 3])] if kc >= 3
                                  else list(bank_guard.get(bi3, [])) +
                                       [("dv", negb_d[h]), ("dv", vt_d[h])])
                            if kc == 0:
                                sw = sw + cool_wall
                                cool_wall = []
                            p.op("tensor",
                                 lambda t, kc=kc, slot=slot, hb=hb, bank=bank:
                                 t.matmul(
                                     bank[:],
                                     kh[hb][:, kc * 128:(kc + 1) * 128],
                                     qh[hb][:, slot * 512:(slot + 1) * 512],
                                     start=True, stop=True),
                                 waits=sw, inc="pe")
                            smm = p.c("pe")
                            wexp = [("pe", smm)]
                            if kc >= nch - 4:
                                r = kc - (nch - 4)
                                p.op("vector",
                                     lambda v, bank=bank, r=r: v.tensor_tensor(
                                         bank[:], bank[:],
                                         mask_s[:, r * 512:(r + 1) * 512],
                                         op=ALU.add),
                                     waits=[("pe", smm), ("kv", mask_ld)])
                                wexp = [("dv", p.c("dv"))]
                            if kc >= 4:
                                wexp.append(("pe", avs[kc - 4]))
                            p.op("scalar",
                                 lambda sc, kc=kc, bank=bank, hb=hb:
                                 sc.activation(
                                     pb[kc % 4][:], bank[:],
                                     AF.Exp, scale=SCALE, bias=negb[hb][:]),
                                 waits=wexp)
                            exps[kc] = p.c("ac")

                          if step >= 2:
                            kc = step - 2

                            def avpair(t, kc=kc, nch=nch, ob=ob, hb=hb):
                                t.matmul(pacc[ob][:],
                                         vtok[hb][:, kc * 128:(kc + 1) * 128],
                                         pb[kc % 4][:],
                                         start=(kc == 0),
                                         stop=(kc == nch - 1))
                                return t.matmul(prow[:], onescol_b[:],
                                                pb[kc % 4][:],
                                                start=(kc == 0),
                                                stop=(kc == nch - 1))
                            aw = [("ac", exps[kc])]
                            if kc == 0:
                                aw += [("dv", out_guard.get(ob, 0)),
                                       ("dv", sum_guard)]
                            p.op("tensor", avpair, waits=aw, inc="pe")
                            avs[kc] = p.c("pe")
                        av_end[h] = avs[nch - 1]
                        p.op("vector", lambda v: v.reciprocal(rs_sb[:],
                                                              prow[:]),
                             waits=[("pe", av_end[h])])
                        sum_guard = rsd = p.c("dv")
                        p.op("tensor",
                             lambda t: t.matmul(psc[1][:], ones1f[:],
                                                rs_sb[:],
                                                start=True, stop=True),
                             waits=[("dv", rsd)], inc="pe")
                        rbd = p.c("pe")
                        p.op("vector", lambda v: v.tensor_copy(r_sb[:],
                                                               psc[1][:]),
                             waits=[("pe", rbd)])
                        rsbd = p.c("dv")
                        for bi3 in range(3):
                            last = max(kc for kc in range(nch)
                                       if kc % 3 == bi3)
                            bank_guard[bi3] = [("ac", exps[last])]
                        bank_guard[1] = bank_guard[1] + [("dv", rsbd)]
                        oi = slot % 2
                        p.op("vector",
                             lambda v, oi=oi, ob=ob: v.scalar_tensor_tensor(
                                 ost[oi][:], pacc[ob][:], 1.0, r_sb[:],
                                 op0=ALU.mult, op1=ALU.mult),
                             waits=[("dv", rsbd),
                                    ("ao", ost_store.get(oi, 0))])
                        out_guard[ob] = p.c("dv")
                        ostd = p.c("dv")
                        ost_store[oi] = p.dma(
                            "gpsimd", "ao",
                            ag_scr[h * 128:(h + 1) * 128,
                                   slot * 512:(slot + 1) * 512],
                            ost[oi][:], waits=[("dv", ostd)])
                    gpump(2)
                gpump(999)
                p.drain_arm()
            p.arms_end()
            if DEBUG_STOP == 1:
                p.op("sync", lambda e: e.nop(), waits=[("ao", p.c("ao"))])
                p.emit(block, gs_s.ap()[0:1, 0:1], gsel_ready)
                return nc

            p.scope = "gfin"
            p.op("tensor", lambda t: t.matmul(prow[:, 0:2], onescol_f[:],
                                              gacc[:, 0:2], start=True,
                                              stop=True),
                 waits=[("dv", p.c("dv")), ("ac", p.c("ac")),
                        ("pe", p.c("pe"))], inc="pe")
            gpe = p.c("pe")
            p.op("vector", lambda v: v.tensor_copy(gsc[:, 0:2], prow[:, 0:2]),
                 waits=[("pe", gpe)])
            p.op("vector", lambda v: v.tensor_scalar_mul(gsc[:, 0:2],
                                                         gsc[:, 0:2],
                                                         1.0 / (D * FF)))
            p.op("vector", lambda v: v.tensor_scalar_add(gsc[:, 2:4],
                                                         gsc[:, 0:2], EPS_B))
            p.op("vector", lambda v: v.tensor_scalar_mul(gsc[:, 4:6],
                                                         gsc[:, 2:4], 0.5))
            p.op("vector", lambda v: v.reciprocal(gsc[:, 2:4], gsc[:, 2:4]))
            for (dst, col) in ((gfc_b, 0), (gpj_b, 1), (gfcinv_b, 2),
                               (gpjinv_b, 3), (thr_fc, 4), (thr_pj, 5)):
                p.op("tensor",
                     lambda t, col=col: t.matmul(pcor[:, 0:1], ones1f[:],
                                                 gsc[0:1, col:col + 1],
                                                 start=True, stop=True),
                     waits=[("dv", p.c("dv"))], inc="pe")
                mm = p.c("pe")
                p.op("vector", lambda v, dst=dst: v.tensor_copy(dst[:],
                                                                pcor[:, 0:1]),
                     waits=[("pe", mm)])
            gbc = p.c("dv")

            p.scope = "p5"
            wo_mm = {}
            attn_free = {}
            for tq in range(4):
                tb = tq * 512
                wag = p.dma("sync", "ag", agsl[:],
                            ag_scr[:, tb:tb + 512]
                            .rearrange("(a p) t -> p a t", p=128),
                            waits=[("pe", p.c("pe")), ("dv", p.c("dv")),
                                   ("ac", p.c("ac"))])
                for tsub in range(4):
                    xw = p.dma("sync", "sx2", xrow[tsub % 2],
                               xown[tb + tsub * 128: tb + (tsub + 1) * 128, :],
                               waits=[("dv", p.c("dv"))])
                    p.op("vector",
                         lambda v, tsub=tsub: v.tensor_copy(h1t[tsub],
                                                            xrow[tsub % 2]),
                         waits=[("sx2", xw), ("h1s", p.c("h1s"))])
                h1i = p.c("dv")
                for ec in range(16):
                    eb = ec % 2
                    wwo = p.dma("gpsimd", "wo", wob[eb],
                                woT[:, ec * 128:(ec + 1) * 128]
                                .rearrange("(a p) e -> p a e", p=128),
                                waits=[("pe", wo_mm.get(eb, 0))])

                    def wmm(t, eb=eb):
                        for dc in range(16):
                            inst = t.matmul(pacc[eb][:], wob[eb][:, dc, :],
                                            agsl[:, dc, :],
                                            start=(dc == 0), stop=(dc == 15))
                        return inst
                    p.op("tensor", wmm,
                         waits=[("wo", wwo), ("ag", wag),
                                ("ac", attn_free.get(eb, 0))], inc="pe")
                    wod = p.c("pe")
                    wo_mm[eb] = wod
                    p.op("scalar",
                         lambda sc, eb=eb: sc.activation(attn_sb[eb],
                                                         pacc[eb][:],
                                                         AF.Copy),
                         waits=[("pe", wod)])
                    attnd = p.c("ac")
                    attn_free[eb] = attnd

                    def trf5(t, eb=eb):
                        for tsub in range(4):
                            inst = t.transpose(
                                ptrf[:, tsub * 128:(tsub + 1) * 128],
                                attn_sb[eb][:, tsub * 128:(tsub + 1) * 128],
                                idf_s[:])
                        return inst
                    p.op("tensor", trf5,
                         waits=[("ac", attnd), ("dv", p.c("dv"))], inc="pe")
                    trd = p.c("pe")
                    for tsub in range(4):
                        p.op("vector",
                             lambda v, tsub=tsub, ec=ec: v.tensor_tensor(
                                 h1t[tsub][:, ec * 128:(ec + 1) * 128],
                                 h1t[tsub][:, ec * 128:(ec + 1) * 128],
                                 ptrf[:, tsub * 128:(tsub + 1) * 128],
                                 op=ALU.add),
                             waits=[("pe", trd), ("dv", h1i)])
                h1d = p.c("dv")
                for tsub in range(4):
                    t128 = tq * 4 + tsub
                    p.dma("scalar", "h1s",
                          xout[tb + tsub * 128: tb + (tsub + 1) * 128, :],
                          h1t[tsub], waits=[("dv", h1d)])
                    for ci in range(4):
                        p.op("vector",
                             lambda v, tsub=tsub, ci=ci: v.bn_stats(
                                 stats[:, ci * 6:(ci + 1) * 6],
                                 h1t[tsub][:, ci * 512:(ci + 1) * 512]),
                             waits=[("dv", h1d)])
                    p.op("vector", lambda v: v.bn_aggr(mv[:], stats[:, 0:24]))
                    agg = p.c("dv")
                    p.op("scalar", lambda sc: sc.activation(rstd[:],
                                                            mv[:, 1:2],
                                                            AF.Ln,
                                                            bias=epsa_t[:]),
                         waits=[("dv", agg)])
                    p.op("scalar", lambda sc: sc.activation(rstd[:], rstd[:],
                                                            AF.Exp,
                                                            scale=-0.5))
                    rst = p.c("ac")
                    p.op("vector",
                         lambda v, tsub=tsub: v.tensor_scalar(
                             xn2, h1t[tsub], mv[:, 0:1], rstd[:],
                             op0=ALU.subtract, op1=ALU.mult),
                         waits=[("ac", rst)])
                    p.op("vector", lambda v: v.reduce_max(
                        gq[:], xn2, axis=AX.X, apply_absolute_value=True))
                    p.op("vector", lambda v: v.reciprocal(s1v[:], gq[:]))
                    p.op("vector", lambda v: v.tensor_scalar_mul(
                        s1v[:], s1v[:], 128.0))
                    p.op("vector", lambda v: v.tensor_scalar_mul(
                        s2v[:], gq[:], 1.0 / 128.0))
                    p.op("vector", lambda v: v.tensor_scalar(
                        clipt, xn2, s1v[:], 128.0, op0=ALU.mult, op1=ALU.min))
                    p.op("vector", lambda v: v.tensor_scalar(
                        xh_tok, clipt, -128.0, s2v[:], op0=ALU.max,
                        op1=ALU.mult))
                    for dq in range(4):
                        def trx(t, dq=dq):
                            for j in range(4):
                                dc = dq * 4 + j
                                inst = t.transpose(
                                    ptr[:, j * 128:(j + 1) * 128],
                                    xh_tok[:, dc * 128:(dc + 1) * 128],
                                    idb_s[:])
                            return inst
                        p.op("tensor", trx, waits=[("dv", p.c("dv"))],
                             inc="pe")
                        trp = p.c("pe")
                        p.op("vector",
                             lambda v, dq=dq, t128=t128: v.tensor_copy(
                                 xfm2[:, dq * 4:(dq + 1) * 4,
                                      t128 * 128:(t128 + 1) * 128],
                                 _r3(ptr[:, 0:512], 128)),
                             waits=[("pe", trp)])
            xfm2d = p.c("dv")
            h1_stores = p.c("h1s")

            p.scope = "p6"
            fcm_done = {}
            gel_last = 0
            hs_done = {}

            def wf_load(oc, wv):
                wst = "wf" if oc % 2 == 0 else "wf2"
                weng = "sync" if oc % 2 == 0 else "scalar"
                return wst, p.dma(
                    weng, wst, wfs[oc % 3][:],
                    wfcT[:, oc * 128:(oc + 1) * 128]
                    .rearrange("(a p) n -> p a n", p=128),
                    waits=[("dv", wv)])
            wldd = {pc: wf_load(pc, xfm2d) for pc in range(3)}
            for oc in range(64):
                ob = oc % 2
                wb3 = oc % 3
                wst, wld = wldd[oc]
                wf = wfs[wb3].rearrange("p a b -> p (a b)")
                p.op("vector",
                     lambda v, wf=wf: v.tensor_scalar(
                         zfc2.rearrange("p a b -> p (a b)"), wf, gfcinv_b[:],
                         -0.5, op0=ALU.mult, op1=ALU.is_le),
                     waits=[(wst, wld), ("dv", gbc),
                            ("pe", p.c("pe") if oc < 2
                             else fcm_done.get(ob, 0))])
                p.op("vector",
                     lambda v, wf=wf, ob=ob: v.scalar_tensor_tensor(
                         tfc[ob].rearrange("p a b -> p (a b)"),
                         wf, thr_fc[:],
                         zfc2.rearrange("p a b -> p (a b)"),
                         op0=ALU.is_ge, op1=ALU.subtract))
                tern_done_oc = p.c("dv")
                if oc + 3 < 64:
                    wldd[oc + 3] = wf_load(oc + 3, tern_done_oc)

                fw = []
                if oc > 0 and oc % 8 == 0:
                    cw2v = p.dma("gpsimd", "cl2", xn_tok[0:1, 0:2048],
                                 kss[0, 0:1, 0:2048],
                                 waits=[("pe", fcm_done[1 - ob])])
                    cw2v = p.dma("gpsimd", "cl2", xn_tok[0:1, 0:2048],
                                 kss[0, 0:1, 0:2048])
                    fw = [("cl2", cw2v)]

                def fcm(t, ob=ob):
                    for dc in range(16):
                        for tq in range(4):
                            inst = t.matmul(pacc[tq][:], tfc[ob][:, dc, :],
                                            xfm2[:, dc,
                                                 tq * 512:(tq + 1) * 512],
                                            start=(dc == 0), stop=(dc == 15))
                    return inst
                p.op("tensor", fcm,
                     waits=[("dv", tern_done_oc), ("ac", gel_last)] + fw,
                     inc="pe")
                fcm_done[ob] = p.c("pe")
                for tq in range(4):
                    hb = tq % 2
                    p.op("scalar",
                         lambda sc, tq=tq, hb=hb: sc.activation(
                             hgel[hb][:], pacc[tq][:], AF.Gelu,
                             scale=gfc_b[:]),
                         waits=[("pe", fcm_done[ob]), ("dv", gbc),
                                ("hs", hs_done.get(hb, 0))])
                    gel_last = p.c("ac")
                    hs_done[hb] = p.dma(
                        "scalar", "hs",
                        h_scr[oc * 128:(oc + 1) * 128,
                              tq * 512:(tq + 1) * 512],
                        hgel[hb][:], waits=[("ac", gel_last)])
            h_stores = p.c("hs")

            p.scope = "p7a"
            ht_free = {0: 0, 1: 0}
            for ti in range(16):
                tb2 = ti % 2
                hbuf = htok if tb2 == 0 else htok2
                hst = "ht" if tb2 == 0 else "ht2"
                w = p.dma("sync", hst, hbuf,
                          h_scr[:, ti * 128:(ti + 1) * 128],
                          waits=[("hs", h_stores), ("dv", ht_free[tb2])] +
                                ([("pe", p.c("pe")), ("gp", p.c("gp"))]
                                 if ti < 2 else []),
                          transpose=True)
                for ci in range(16):
                    p.op("vector",
                         lambda v, ci=ci, hbuf=hbuf: v.bn_stats(
                             stats[:, ci * 6:(ci + 1) * 6],
                             hbuf[:, ci * 512:(ci + 1) * 512]),
                         waits=[(hst, w)])
                p.op("vector", lambda v: v.bn_aggr(mv[:], stats[:]))
                p.op("vector", lambda v, hbuf=hbuf: v.reduce_max(
                    maxh[:], hbuf, axis=AX.X))
                p.op("vector", lambda v, hbuf=hbuf: v.tensor_reduce(
                    minh[:], hbuf, axis=AX.X, op=ALU.min))
                ht_free[tb2] = p.c("dv")
                p.op("scalar", lambda sc: sc.activation(rstd[:], mv[:, 1:2],
                                                        AF.Ln,
                                                        bias=epsb_t[:]),
                     waits=[("dv", ht_free[tb2])])
                p.op("scalar", lambda sc: sc.activation(rstd[:], rstd[:],
                                                        AF.Exp, scale=-0.5))
                rst = p.c("ac")
                p.op("vector", lambda v: v.tensor_scalar(
                    ga[:], maxh[:], mv[:, 0:1], None, op0=ALU.subtract),
                     waits=[("ac", rst)])
                p.op("vector", lambda v: v.tensor_scalar(
                    gb[:], minh[:], mv[:, 0:1], -1.0, op0=ALU.subtract,
                    op1=ALU.mult))
                p.op("vector", lambda v: v.tensor_tensor(gq[:], ga[:], gb[:],
                                                         op=ALU.max))
                p.op("vector", lambda v: v.tensor_scalar(gq[:], gq[:],
                                                         rstd[:], None,
                                                         op0=ALU.mult))
                p.op("vector", lambda v: v.reciprocal(s1v[:], gq[:]))
                p.op("vector", lambda v: v.tensor_scalar_mul(s1v[:], s1v[:],
                                                             128.0))
                p.op("vector", lambda v: v.tensor_scalar_mul(s2v[:], gq[:],
                                                             1.0 / 128.0))
                p.op("vector", lambda v: v.tensor_scalar(s1v[:], s1v[:],
                                                         s2v[:], rstd[:],
                                                         op0=ALU.mult,
                                                         op1=ALU.mult))
                p.op("vector",
                     lambda v, ti=ti: v.tensor_scalar(
                         svec[:, ti:ti + 1], s1v[:], gpj_b[:], None,
                         op0=ALU.mult),
                     waits=[("dv", gbc)])
                p.op("vector",
                     lambda v, ti=ti: v.tensor_scalar(
                         msv[:], svec[:, ti:ti + 1], mv[:, 0:1], -1.0,
                         op0=ALU.mult, op1=ALU.mult))
                msd = p.c("dv")
                p.op("tensor", lambda t: t.matmul(prow[:, 0:128], msv[:],
                                                  idf_s[:],
                                                  start=True, stop=True),
                     waits=[("dv", msd)], inc="pe")
                msp = p.c("pe")
                p.op("vector",
                     lambda v, ti=ti: v.tensor_copy(
                         msrow[:, ti * 128:(ti + 1) * 128], prow[:, 0:128]),
                     waits=[("pe", msp)])
            p7a_d = p.c("dv")

            p.scope = "p7b"
            o2_done = {}
            o2_all = 0
            wp_done = {}
            for eb in range(4):
                ebc = eb * 512
                for half in range(8):
                    hb3 = half % 3
                    wst = "wp" if half % 2 == 0 else "wp2"
                    weng = "sync" if half % 2 == 0 else "scalar"
                    exw = []
                    if eb == 0 and half < 2:
                        exw = [("h1s", h1_stores)]
                    elif eb == 0 and half == 2:
                        exw = [("hs", h_stores)]
                    wld = p.dma(weng, wst, wps[hb3],
                                wpjT[half * 1024:(half + 1) * 1024,
                                     ebc:ebc + 512]
                                .rearrange("(a p) n -> p a n", p=128),
                                waits=[("dv", wp_done.get(hb3, 0))] + exw)
                    wfp = wps[hb3].rearrange("p a b -> p (a b)")
                    p.op("vector",
                         lambda v, wfp=wfp: v.tensor_scalar(
                             zpj2.rearrange("p a b -> p (a b)"), wfp,
                             gpjinv_b[:], -0.5, op0=ALU.mult, op1=ALU.is_le),
                         waits=[(wst, wld), ("dv", gbc),
                                ("pe", p.c("pe") if half < 3 else 0)])
                    p.op("vector",
                         lambda v, wfp=wfp, half=half:
                         v.scalar_tensor_tensor(
                             tpj[:, half * 8:(half + 1) * 8, :].rearrange(
                                 "p a b -> p (a b)"),
                             wfp, thr_pj[:],
                             zpj2.rearrange("p a b -> p (a b)"),
                             op0=ALU.is_ge, op1=ALU.subtract))
                    wp_done[hb3] = p.c("dv")
                ternd = p.c("dv")

                def csf(t):
                    for oc in range(64):
                        inst = t.matmul(prow[:], onescol_b[:], tpj[:, oc, :],
                                        start=(oc == 0), stop=(oc == 63))
                    return inst
                p.op("tensor", csf, waits=[("dv", ternd)], inc="pe")
                csd = p.c("pe")
                p.op("vector", lambda v: v.tensor_copy(crow[:], prow[:]),
                     waits=[("pe", csd)])
                crd = p.c("dv")

                for tq in range(4):
                    hwv = [None] * 64
                    pe_oc = [None] * 64
                    for pre in range(3):
                        hwv[pre] = p.dma(
                            "gpsimd", f"htl{pre}", htile[pre][:],
                            h_scr[pre * 128:(pre + 1) * 128,
                                  tq * 512:(tq + 1) * 512],
                            waits=[("hs", h_stores)])
                    pw = []
                    if (eb * 4 + tq) % 2 == 0 and eb + tq > 0:
                        cw2v = p.dma("gpsimd", "cl2", xn_tok[0:1, 0:2048],
                                     kss[0, 0:1, 0:2048],
                                     waits=[("pe", p.c("pe"))])
                        cw2v = p.dma("gpsimd", "cl2", xn_tok[0:1, 0:2048],
                                     kss[0, 0:1, 0:2048])
                        pw = [("cl2", cw2v)]
                    for oc in range(64):
                        def pjm(t, oc=oc):
                            for ts4 in range(4):
                                inst = t.matmul(
                                    pacc[ts4][:],
                                    htile[oc % 3][:,
                                                  ts4 * 128:(ts4 + 1) * 128],
                                    tpj[:, oc, :],
                                    start=(oc == 0), stop=(oc == 63))
                            return inst
                        p.op("tensor", pjm,
                             waits=[(f"htl{oc % 3}", hwv[oc]),
                                    ("dv", o2_all if oc == 0 else 0)] +
                                   (pw if oc == 0 else []),
                             inc="pe")
                        pe_oc[oc] = p.c("pe")
                        if oc + 3 < 64:
                            hwv[oc + 3] = p.dma(
                                "gpsimd", f"htl{oc % 3}", htile[oc % 3][:],
                                h_scr[(oc + 3) * 128:(oc + 4) * 128,
                                      tq * 512:(tq + 1) * 512],
                                waits=[("pe", pe_oc[oc])])
                    pjd = p.c("pe")
                    for ts4 in range(4):
                        t128 = tq * 4 + ts4
                        p.op("tensor",
                             lambda t, t128=t128: t.matmul(
                                 pcor[:],
                                 msrow[0:1, t128 * 128:(t128 + 1) * 128],
                                 crow[:], start=True, stop=True),
                             waits=[("dv", p.c("dv")), ("pe", pjd)], inc="pe")
                        cord = p.c("pe")
                        p.op("vector", lambda v: v.tensor_copy(csb[:],
                                                               pcor[:]),
                             waits=[("pe", cord)])
                        csbd = p.c("dv")
                        oi = ts4 % 2
                        p.op("vector",
                             lambda v, ts4=ts4, t128=t128, oi=oi:
                             v.scalar_tensor_tensor(
                                 o2sb[oi][:], pacc[ts4][:],
                                 svec[:, t128:t128 + 1], csb[:],
                                 op0=ALU.mult, op1=ALU.add),
                             waits=[("dv", csbd),
                                    ("o2s", o2_done.get(oi, 0)),
                                    ("pe", pjd)])
                        o2d = p.c("dv")
                        o2_done[oi] = p.dma(
                            "gpsimd", "o2s",
                            xout[t128 * 128:(t128 + 1) * 128, ebc:ebc + 512],
                            o2sb[oi][:],
                            waits=[("dv", o2d), ("h1s", h1_stores)],
                            accum_op=ALU.add)
                    o2_all = p.c("dv")

            p.op("sync", lambda e: e.nop(),
                 waits=[("o2s", p.c("o2s")), ("h1s", h1_stores),
                        ("hs", p.c("hs"))])
            if p.c("cl2") > 0:
                p.op("gpsimd", lambda e: e.nop(),
                     waits=[("cl2", p.c("cl2"))], _isdma=True)
            p.emit(block, gs_s.ap()[0:1, 0:1], gsel_ready)
    return nc



def _host_constants():
    inv = (np.float32(1.0) /
           np.float32(10000.0) ** (np.arange(0, HD, 2, dtype=np.float32) /
                                   np.float32(HD)))
    t = np.arange(S, dtype=np.float32)
    freqs = np.outer(t, inv).astype(np.float32)
    emb = np.concatenate([freqs, freqs], axis=1)
    cosf = np.ascontiguousarray(np.cos(emb).T.astype(np.float32))
    sinf = np.ascontiguousarray(np.sin(emb).T.astype(np.float32))
    rt = np.zeros((HD, HD), dtype=ml_dtypes.bfloat16)
    for i in range(64):
        rt[i + 64, i] = -1
        rt[i, i + 64] = 1
    idb = np.eye(128, dtype=ml_dtypes.bfloat16)
    idf = np.eye(128, dtype=np.float32)
    masks = np.zeros((128, 4 * 512), dtype=np.float32)
    pp = np.arange(128)[:, None]
    nn = np.arange(512)[None, :]
    for r in range(4):
        masks[:, r * 512:(r + 1) * 512] = np.where(r * 128 + pp <= nn, 0.0, NEG)
    return cosf, sinf, rt, idb, idf, masks


_NC_CACHE = {}
DEBUG_STOP = 0
COOL_N = 3
COOL_EVERY = 1


def kernel(L0_attn_x, L0_attn_Wq, L0_attn_Wk, L0_attn_Wv, L0_attn_Wo,
           L0_ffn_w_fc, L0_ffn_w_proj):
    x = np.ascontiguousarray(np.asarray(L0_attn_x, dtype=np.float32))
    Wq = np.asarray(L0_attn_Wq, dtype=np.float32)
    Wk = np.asarray(L0_attn_Wk, dtype=np.float32)
    Wv = np.asarray(L0_attn_Wv, dtype=np.float32)
    Wo = np.asarray(L0_attn_Wo, dtype=np.float32)
    wfc = np.asarray(L0_ffn_w_fc, dtype=np.float32)
    wpj = np.asarray(L0_ffn_w_proj, dtype=np.float32)
    cosf, sinf, rt, idb, idf, masks = _host_constants()

    if "nc" not in _NC_CACHE:
        _NC_CACHE["nc"] = build_fused()
    nc = _NC_CACHE["nc"]

    wqTh = np.ascontiguousarray(Wq.T)
    wkTh = np.ascontiguousarray(Wk.T)
    wvTh = np.ascontiguousarray(Wv.T)
    woTh = np.ascontiguousarray(Wo.T)
    wfcTh = np.ascontiguousarray(wfc.T)
    wpjTh = np.ascontiguousarray(wpj.T)
    in_maps = []
    for c in range(8):
        b, g = c // 2, c % 2
        if g == 0:
            xo = np.concatenate([x[b, 0:1024], x[b, 3072:4096]], axis=0)
        else:
            xo = x[b, 1024:3072]
        in_maps.append({
            "xb": x[b],
            "xown": np.ascontiguousarray(xo),
            "wqT": wqTh, "wkT": wkTh, "wvT": wvTh, "woT": woTh,
            "wfcT": wfcTh, "wpjT": wpjTh,
            "cosf": cosf, "sinf": sinf, "rtc": rt, "idbi": idb, "idfi": idf,
            "masks": masks,
            "gseli": np.array([[g]], dtype=np.int32),
        })
    res = run_bass_kernel_spmd(nc, in_maps, core_ids=list(range(8)))
    out = np.empty((B, S, D), dtype=np.float32)
    for c in range(8):
        b, g = c // 2, c % 2
        xo = res.results[c]["xout"]
        if g == 0:
            out[b, 0:1024] = xo[0:1024]
            out[b, 3072:4096] = xo[1024:2048]
        else:
            out[b, 1024:3072] = xo
    return out


# version 13
# speedup vs baseline: 1.0485x; 1.0061x over previous
import math
from contextlib import ExitStack

import numpy as np
import ml_dtypes
import concourse.bass as bass
import concourse.mybir as mybir
from concourse.bass_utils import run_bass_kernel_spmd

F32 = mybir.dt.float32
BF16 = mybir.dt.bfloat16
I32 = mybir.dt.int32
AF = mybir.ActivationFunctionType
ALU = mybir.AluOpType
AX = mybir.AxisListType

B, S, D, H, HD, FF = 4, 4096, 2048, 16, 128, 8192
TOWN = 2048
SCALE = 1.0 / math.sqrt(HD)
EPS_A = 1e-5
EPS_B = 1e-6
NEG = -1.0e9

ENGS = ("sync", "gpsimd", "vector", "scalar", "tensor")
SELF_SYNC = {"vector": "dv", "scalar": "ac", "gpsimd": "gp"}
COMPUTE_SEMS = ("dv", "ac", "pe", "gp")
SEM_OWNER = {"dv": "vector", "ac": "scalar", "pe": "tensor", "gp": "gpsimd"}

ARM_PARAMS = [
    dict(extent=4096,
         sts=[(0, 16, 4, [(0, 0), (1, 1)]), (2048, 16, 4, [(2, 2), (3, 3)])],
         qtiles=[(0, 0), (1, 1), (2, 6), (3, 7)]),
    dict(extent=3072,
         sts=[(0, 16, 4, [(2, 0), (3, 1)]), (2048, 8, 2, [(0, 2), (1, 3)])],
         qtiles=[(0, 2), (1, 3), (2, 4), (3, 5)]),
]


class Prog:

    def __init__(self, nc, ctx):
        self.nc = nc
        self.ctx = ctx
        self.segments = [("common", {e: [] for e in ENGS})]
        self.cnt = {}
        self.sems = {}
        self.stream_eng = {}
        self.arm_state = None
        self.arm_pads = None
        self.scope = "init"

    def mksem(self, name):
        self.sems[name] = self.ctx.enter_context(self.nc.semaphore(name))
        self.cnt[name] = 0

    def mksems(self, names):
        for name in names:
            self.mksem(name)

    def _cur(self):
        seg = self.segments[-1]
        return seg[1] if seg[0] == "common" else seg[1][-1]

    def arms_begin(self):
        assert self.arm_state is None
        self.segments.append(("arms", []))
        self.arm_state = (dict(self.cnt), [])

    def arm(self):
        snap, ends = self.arm_state
        if self.segments[-1][1]:
            ends.append(dict(self.cnt))
        self.cnt = dict(snap)
        self.segments[-1][1].append({e: [] for e in ENGS})

    def drain_arm(self):
        snap, _ = self.arm_state
        for st, eng in list(self.stream_eng.items()):
            v = self.cnt.get(st, 0)
            if v > snap.get(st, 0):
                self.op(eng, lambda e: e.nop(), waits=[(st, v)], _isdma=True)

    def arms_end(self):
        snap, ends = self.arm_state
        ends.append(dict(self.cnt))
        final = {k: max(e.get(k, 0) for e in ends) for k in self.cnt}
        self.arm_pads = [{k: final[k] - e.get(k, 0) for k in final
                          if k in COMPUTE_SEMS} for e in ends]
        self.arm_ends = ends
        self.cnt = final
        self.arm_state = None
        self.segments.append(("common", {e: [] for e in ENGS}))

    def op(self, eng, fn, waits=(), inc=None, incn=1, _isdma=False):
        waits = list(waits)
        if not _isdma and eng in SELF_SYNC:
            own = SELF_SYNC[eng]
            if inc is None:
                inc, incn = own, 1
            assert inc == own, (eng, inc)
            waits.append((own, self.cnt[own]))
        self._cur()[eng].append((tuple(waits), fn, inc, incn, self.scope))
        if inc is not None:
            self.cnt[inc] = self.cnt.get(inc, 0) + incn
            return self.cnt[inc]
        return None

    def dma(self, eng, stream, out, in_, waits=(), **kw):
        if stream not in self.sems:
            self.mksem(stream)
        assert self.stream_eng.setdefault(stream, eng) == eng, stream
        waits = list(waits) + [(stream, self.cnt.get(stream, 0))]
        return self.op(eng, lambda e: e.dma_start(out=out, in_=in_, **kw),
                       waits=waits, inc=stream, incn=16, _isdma=True)

    def c(self, sem):
        return self.cnt.get(sem, 0)

    def emit(self, block, gsel_ap, gsel_ready):
        sems = self.sems

        cur_scope = [None, 0]

        def emit_ops(eng, ops, hw):
            for waits, fn, inc, incn, scope in ops:
                if scope != cur_scope[0]:
                    if cur_scope[0] is not None:
                        self.nc.leave_named_scope(cur_scope[0], cur_scope[1], False)
                    sid, _ = self.nc.enter_named_scope(scope, False)
                    cur_scope[0], cur_scope[1] = scope, sid
                for (s, v) in waits:
                    if v > 0 and hw.get(s, 0) < v:
                        eng.wait_ge(sems[s], v)
                        hw[s] = v
                inst = fn(eng)
                if inc is not None:
                    inst.then_inc(sems[inc], incn)

        for eng_name in ENGS:
            def body(eng, eng_name=eng_name):
                hw = {}
                post_arms = False
                for seg in self.segments:
                    if seg[0] == "common":
                        if post_arms:
                            eng.wait_ge(sems["join"], 5)
                            post_arms = False
                        emit_ops(eng, seg[1][eng_name], hw)
                        continue
                    if hw.get("gsel_ld", 0) < gsel_ready:
                        eng.wait_ge(sems["gsel_ld"], gsel_ready)
                        hw["gsel_ld"] = gsel_ready
                    with eng.register(f"gsel_{eng_name}") as gr:
                        eng.reg_load(gr, gsel_ap)
                        for i in eng.Switch(eng.snap(gr, min_val=0, max_val=1), 2):
                            emit_ops(eng, seg[1][i][eng_name], dict(hw))
                            for sem, pad in self.arm_pads[i].items():
                                if pad > 0 and SEM_OWNER[sem] == eng_name:
                                    eng.wait_ge(sems[sem],
                                                self.arm_ends[i][sem])
                                    eng.sem_inc(sems[sem], pad)
                            eng.sem_inc(sems["join"], 1)
                    hw = {}
                    post_arms = True
                if cur_scope[0] is not None:
                    self.nc.leave_named_scope(cur_scope[0], cur_scope[1], False)
                    cur_scope[0] = None
            getattr(block, eng_name)(body)


def _r3(ap2d, b):
    return ap2d.rearrange("p (a b) -> p a b", b=b)


def build_fused():
    nc = bass.Bass("TRN2", target_bir_lowering=False)
    ap = lambda t: t.ap()
    xb = ap(nc.dram_tensor("xb", [S, D], F32, kind="ExternalInput"))
    xown = ap(nc.dram_tensor("xown", [TOWN, D], F32, kind="ExternalInput"))
    wqT = ap(nc.dram_tensor("wqT", [D, D], F32, kind="ExternalInput"))
    wkT = ap(nc.dram_tensor("wkT", [D, D], F32, kind="ExternalInput"))
    wvT = ap(nc.dram_tensor("wvT", [D, D], F32, kind="ExternalInput"))
    woT = ap(nc.dram_tensor("woT", [D, D], F32, kind="ExternalInput"))
    wfcT = ap(nc.dram_tensor("wfcT", [D, FF], F32, kind="ExternalInput"))
    wpjT = ap(nc.dram_tensor("wpjT", [FF, D], F32, kind="ExternalInput"))
    cosf = ap(nc.dram_tensor("cosf", [HD, S], F32, kind="ExternalInput"))
    sinf = ap(nc.dram_tensor("sinf", [HD, S], F32, kind="ExternalInput"))
    rtc = ap(nc.dram_tensor("rtc", [HD, HD], BF16, kind="ExternalInput"))
    idbi = ap(nc.dram_tensor("idbi", [128, 128], BF16, kind="ExternalInput"))
    idfi = ap(nc.dram_tensor("idfi", [128, 128], F32, kind="ExternalInput"))
    masks = ap(nc.dram_tensor("masks", [128, 4 * 512], F32, kind="ExternalInput"))
    gseli = ap(nc.dram_tensor("gseli", [1, 1], I32, kind="ExternalInput"))

    xout = ap(nc.dram_tensor("xout", [TOWN, D], F32, kind="ExternalOutput"))

    qss = ap(nc.dram_tensor("qss", [H, HD, TOWN], BF16))
    kss = ap(nc.dram_tensor("kss", [H, HD, S], BF16))
    vss = ap(nc.dram_tensor("vss", [H, HD, S], BF16))
    ag_scr = ap(nc.dram_tensor("ag_scr", [D, TOWN], BF16))
    h_scr = ap(nc.dram_tensor("h_scr", [FF, TOWN], BF16))

    with ExitStack() as ctx:
        sb = lambda name, shape, dt: ctx.enter_context(nc.sbuf_tensor(name, shape, dt))
        psa = lambda name, shape, dt: ctx.enter_context(nc.psum_tensor(name, shape, dt))

        rt_s = sb("rt_s", [HD, HD], BF16)
        idb_s = sb("idb_s", [128, 128], BF16)
        idf_s = sb("idf_s", [128, 128], F32)
        gs_s = sb("gs_s", [1, 1], I32)
        ones1f = sb("ones1f", [1, 128], F32)
        onescol_f = sb("onescol_f", [128, 1], F32)
        onescol_b = sb("onescol_b", [128, 1], BF16)
        epsa_t = sb("epsa_t", [128, 1], F32)
        epsb_t = sb("epsb_t", [128, 1], F32)
        gdump = sb("gdump", [128, 1024], F32)
        gtmp = sb("gtmp", [128, 1], F32)
        gacc = sb("gacc", [128, 2], F32)
        gsc = sb("gsc", [1, 8], F32)
        gfc_b = sb("gfc_b", [128, 1], F32)
        gpj_b = sb("gpj_b", [128, 1], F32)
        gfcinv_b = sb("gfcinv_b", [128, 1], F32)
        gpjinv_b = sb("gpjinv_b", [128, 1], F32)
        thr_fc = sb("thr_fc", [128, 1], F32)
        thr_pj = sb("thr_pj", [128, 1], F32)
        stats = sb("stats", [128, 96], F32)
        mv = sb("mv", [128, 2], F32)
        rstd = sb("rstd", [128, 1], F32)
        xn_tok = sb("xn_tok", [128, D], BF16)
        raw = [sb(f"raw{i}", [128, 512], BF16) for i in range(2)]
        t1b = [sb("t1b0", [128, 512], F32)] * 2
        t2b = [sb("t2b0", [128, 512], F32)] * 2
        stg = [sb(f"stg{i}", [128, 512], BF16) for i in range(4)]
        mx8 = sb("mx8", [1, 16], F32)
        sc1 = sb("sc1", [1, 4], F32)
        negb = [sb(f"negb{i}", [128, 1], F32) for i in range(2)]
        rs_sb = sb("rs_sb", [1, 512], F32)
        r_sb = sb("r_sb", [128, 512], F32)
        ost = [sb(f"ost{i}", [128, 512], BF16) for i in range(2)]
        gq = sb("gq", [128, 1], F32)
        s1v = sb("s1v", [128, 1], F32)
        s2v = sb("s2v", [128, 1], F32)
        hgel = [sb(f"hgel{i}", [128, 512], BF16) for i in range(2)]
        maxh = sb("maxh", [128, 1], F32)
        minh = sb("minh", [128, 1], F32)
        ga = sb("ga", [128, 1], F32)
        gb = sb("gb", [128, 1], F32)
        svec = sb("svec", [128, 16], F32)
        msv = sb("msv", [128, 1], F32)
        msrow = sb("msrow", [1, TOWN], F32)
        crow = sb("crow", [1, 512], F32)
        htile = [sb(f"htile{i}", [128, 512], BF16) for i in range(3)]
        csb = sb("csb", [128, 512], F32)
        o2sb = [sb(f"o2sb{i}", [128, 512], F32) for i in range(2)]

        arena = sb("arena", [128, 49152], BF16)
        ar = arena.ap()
        xn_fm = _r3(ar[:, 0:32768], 2048)
        wbf = [_r3(ar[:, 32768 + i * 8192: 32768 + (i + 1) * 8192], 512)
               for i in range(2)]
        kh = [ar[:, 0:4096], ar[:, 4096:8192]]
        vh = [ar[:, 8192:12288], ar[:, 12288:16384]]
        qh = [ar[:, 16384:18432], ar[:, 18432:20480]]
        vtok = [ar[:, 20480:24576], ar[:, 30720:34816]]
        pb = [ar[:, 24576 + i * 512: 24576 + (i + 1) * 512] for i in range(4)]
        sqb = ar[:, 26624:30720]
        mask_s = ar[:, 45056:49152].bitcast(F32)
        xfm2 = _r3(ar[:, 0:32768], 2048)
        agsl = _r3(ar[:, 32768:40960], 512)
        wob = [_r3(ar[:, 40960 + i * 2048: 40960 + (i + 1) * 2048], 128)
               for i in range(2)]
        tfc = [_r3(ar[:, 45056 + i * 2048: 45056 + (i + 1) * 2048], 128)
               for i in range(2)]
        xh_tok = ar[:, 45056:47104]
        attn_sb = [ar[:, 47104 + i * 1024: 47104 + (i + 1) * 1024].bitcast(F32)
                   for i in range(2)]
        htok = ar[:, 32768:40960]
        htok2 = ar[:, 40960:49152]
        tpj = _r3(ar[:, 0:32768], 512)

        arfa = sb("arfa", [128, 8192], F32)
        cos_s = arfa.ap()[:, 0:2048]
        sin_s = arfa.ap()[:, 2048:4096]
        xt = [arfa.ap()[:, 4096:6144], arfa.ap()[:, 6144:8192]]
        h1t = [arfa.ap()[:, i * 2048:(i + 1) * 2048] for i in range(4)]

        arfb = sb("arfb", [128, 8192], F32)
        xn2 = arfb.ap()[:, 0:2048]
        clipt = arfb.ap()[:, 2048:4096]
        xrow = [arfb.ap()[:, 4096 + i * 2048: 4096 + (i + 1) * 2048]
                for i in range(2)]
        wfs = [_r3(arfb.ap()[:, i * 2048:(i + 1) * 2048], 128)
               for i in range(3)]
        zfc2 = _r3(arfb.ap()[:, 6144:8192], 128)
        zpj2 = _r3(arfb.ap()[:, 4096:8192], 512)
        gdump2 = arfb.ap()[:, 0:1024]
        wps = [_r3(arfa.ap()[:, 0:4096], 512),
               _r3(arfa.ap()[:, 4096:8192], 512),
               _r3(arfb.ap()[:, 0:4096], 512)]
        grow_t = arfb.ap()[0:1, 0:1024]
        grow_a = [arfb.ap()[0:1, 1024 + i * 1024: 1024 + (i + 1) * 1024]
                  for i in range(2)]

        pacc = [psa(f"pacc{i}", [128, 512], F32) for i in range(4)]
        ptr = psa("ptr", [128, 512], BF16)
        psc = [psa(f"psc{i}", [128, 512], F32) for i in range(2)]
        prow = psa("prow", [1, 512], F32)
        ptrf = psc[0]
        pcor = psc[1]

        p = Prog(nc, ctx)
        p.mksems(("dv", "ac", "pe", "gp", "join"))

        gst = {}

        def gpump_reset():
            gst["i"] = 0
            gst["add"] = p.c("dv")
            gst["abs"] = p.c("dv")
            gst["abs0"] = gst["abs1"] = p.c("dv")

        def gpump(n):
            for _ in range(n):
                i = gst["i"]
                if i >= 256:
                    return
                gst["i"] += 1
                gi, si = (0, i) if i < 128 else (1, i - 128)
                src = (wfcT if gi == 0 else wpjT).rearrange("a b -> (a b)")
                blk = src[si * 131072:(si + 1) * 131072].rearrange(
                    "(q n) -> q n", n=1024)
                gpar = i % 2
                gbuf = gdump[:] if gpar == 0 else gdump2
                gstn = "gm" if gpar == 0 else "gm2"
                w = p.dma("scalar", gstn, gbuf, blk,
                          waits=[("dv", gst["abs%d" % gpar])])
                gst["abs%d" % gpar] = gst["abs"] = p.op(
                    "vector",
                    lambda v, gbuf=gbuf: v.tensor_reduce(
                        gtmp[:], gbuf, axis=AX.X, op=ALU.add,
                        apply_absolute_value=True),
                    waits=[(gstn, w)])
                gst["add"] = p.op(
                    "vector",
                    lambda v, gi=gi: v.tensor_tensor(
                        gacc[:, gi:gi + 1], gacc[:, gi:gi + 1], gtmp[:],
                        op=ALU.add))

        with nc.Block() as block:
            p.scope = "c0"
            p.dma("sync", "gsel_ld", gs_s[:], gseli)
            gsel_ready = 16
            p.dma("sync", "consts", rt_s[:], rtc)
            p.dma("sync", "consts", idb_s[:], idbi)
            p.dma("sync", "consts", idf_s[:], idfi)
            consts = p.c("consts")
            p.op("vector", lambda v: v.memset(ones1f[:], 1.0))
            p.op("vector", lambda v: v.memset(onescol_f[:], 1.0))
            p.op("vector", lambda v: v.memset(onescol_b[:], 1.0))
            p.op("vector", lambda v: v.memset(epsa_t[:], EPS_A))
            p.op("vector", lambda v: v.memset(epsb_t[:], EPS_B))
            p.op("vector", lambda v: v.memset(gacc[:], 0.0))
            memsets = p.c("dv")
            gst["ms"] = memsets

            p.arms_begin()
            for g in range(2):
                p.arm()
                prm = ARM_PARAMS[g]
                gpump_reset()
                wr_done = {}
                xn_last = {}
                bank_free = {}
                ost_store = {}

                p.scope = "p2"
                for (t0, n_xt, n_kv_tq, own) in prm["sts"]:
                    p.dma("sync", "cs", cos_s, cosf[:, t0:t0 + 2048],
                          waits=[("gp", p.c("gp")), ("dv", p.c("dv"))])
                    csw = p.dma("sync", "cs", sin_s, sinf[:, t0:t0 + 2048])
                    for ti in range(n_xt):
                        buf = xt[ti % 2]
                        w = p.dma("sync", "sx", buf,
                                  xb[t0 + ti * 128: t0 + (ti + 1) * 128, :],
                                  waits=[("dv", xn_last.get(ti % 2, 0))])
                        for ci in range(4):
                            p.op("vector",
                                 lambda v, buf=buf, ci=ci: v.bn_stats(
                                     stats[:, ci * 6:(ci + 1) * 6],
                                     buf[:, ci * 512:(ci + 1) * 512]),
                                 waits=[("sx", w), ("dv", memsets)])
                        p.op("vector", lambda v: v.bn_aggr(mv[:], stats[:, 0:24]))
                        agg = p.c("dv")
                        p.op("scalar",
                             lambda sc: sc.activation(rstd[:], mv[:, 1:2],
                                                      AF.Ln, bias=epsa_t[:]),
                             waits=[("dv", agg)])
                        p.op("scalar",
                             lambda sc: sc.activation(rstd[:], rstd[:], AF.Exp,
                                                      scale=-0.5))
                        rst = p.c("ac")
                        p.op("vector",
                             lambda v, buf=buf: v.tensor_scalar(
                                 xn_tok[:], buf, mv[:, 0:1], rstd[:],
                                 op0=ALU.subtract, op1=ALU.mult),
                             waits=[("ac", rst)])
                        xn_last[ti % 2] = p.c("dv")
                        for dq in range(4):
                            def trf(t, dq=dq):
                                for j in range(4):
                                    dc = dq * 4 + j
                                    inst = t.transpose(
                                        ptr[:, j * 128:(j + 1) * 128],
                                        xn_tok[:, dc * 128:(dc + 1) * 128],
                                        idb_s[:])
                                return inst
                            p.op("tensor", trf,
                                 waits=[("dv", p.c("dv")), ("consts", consts)],
                                 inc="pe")
                            trp = p.c("pe")
                            p.op("vector",
                                 lambda v, dq=dq, ti=ti: v.tensor_copy(
                                     xn_fm[:, dq * 4:(dq + 1) * 4,
                                           ti * 128:(ti + 1) * 128],
                                     _r3(ptr[:, 0:512], 128)),
                                 waits=[("pe", trp)])
                        gpump(2)
                    xnfm_done = p.c("dv")
                    mm_q = {}
                    for (wsrc, scr, dorope, tqs) in (
                            (wkT, kss, True,
                             [(tq, tq) for tq in range(n_kv_tq)]),
                            (wvT, vss, False,
                             [(tq, tq) for tq in range(n_kv_tq)]),
                            (wqT, qss, True, own)):
                        for qtr in range(4):
                            wb = qtr % 2
                            wl = p.dma("gpsimd", "wl", wbf[wb],
                                       wsrc[:, qtr * 512:(qtr + 1) * 512]
                                       .rearrange("(a p) e -> p a e", p=128),
                                       waits=[("pe", mm_q.get(wb, 0))])
                            for ec4 in range(4):
                                h = qtr * 4 + ec4
                                for (tq, slot) in tqs:
                                    bidx = (ec4 * 4 + tq) % 4
                                    bank = pacc[bidx]

                                    def mmf(t, ec4=ec4, tq=tq, bank=bank,
                                            wb=wb):
                                        for dc in range(16):
                                            inst = t.matmul(
                                                bank[:],
                                                wbf[wb][:, dc,
                                                        ec4 * 128:
                                                        (ec4 + 1) * 128],
                                                xn_fm[:, dc,
                                                      tq * 512:(tq + 1) * 512],
                                                start=(dc == 0),
                                                stop=(dc == 15))
                                        return inst
                                    p.op("tensor", mmf,
                                         waits=[("wl", wl), ("dv", xnfm_done),
                                                ("dv", bank_free.get(bidx, 0))],
                                         inc="pe")
                                    mmn = p.c("pe")
                                    mm_q[wb] = mmn
                                    si = (h * 4 + tq) % 2
                                    so = (h * 4 + tq) % 4
                                    if dorope:
                                        p.op("vector",
                                             lambda v, bank=bank, si=si:
                                             v.tensor_copy(raw[si][:], bank[:]),
                                             waits=[("pe", mmn)])
                                        rawn = p.c("dv")
                                        bank_free[bidx] = rawn
                                        p.op("tensor",
                                             lambda t, si=si: t.matmul(
                                                 psc[1][:], rt_s[:],
                                                 raw[si][:],
                                                 start=True, stop=True),
                                             waits=[("dv", rawn)], inc="pe")
                                        rotn = p.c("pe")
                                        p.op("gpsimd",
                                             lambda gg, si=si, tq=tq:
                                             gg.tensor_tensor(
                                                 t1b[si][:], raw[si][:],
                                                 cos_s[:, tq * 512:
                                                       (tq + 1) * 512],
                                                 op=ALU.mult),
                                             waits=[("dv", rawn), ("cs", csw)])
                                        gpn = p.c("gp")
                                        p.op("vector",
                                             lambda v, si=si, tq=tq:
                                             v.tensor_tensor(
                                                 t2b[si][:], psc[1][:],
                                                 sin_s[:, tq * 512:
                                                       (tq + 1) * 512],
                                                 op=ALU.mult),
                                             waits=[("pe", rotn), ("cs", csw)])
                                        p.op("vector",
                                             lambda v, si=si, so=so:
                                             v.tensor_tensor(
                                                 stg[so][:], t2b[si][:],
                                                 t1b[si][:], op=ALU.add),
                                             waits=[("gp", gpn),
                                                    ("wr", wr_done.get(so, 0))])
                                    else:
                                        p.op("vector",
                                             lambda v, bank=bank, so=so:
                                             v.tensor_copy(stg[so][:],
                                                           bank[:]),
                                             waits=[("pe", mmn),
                                                    ("wr", wr_done.get(so, 0))])
                                        bank_free[bidx] = p.c("dv")
                                    stn = p.c("dv")
                                    dst = (scr[h, :,
                                               slot * 512:(slot + 1) * 512]
                                           if wsrc is wqT else
                                           scr[h, :, t0 + tq * 512:
                                               t0 + (tq + 1) * 512])
                                    wr_done[so] = p.dma(
                                        "gpsimd", "wr", dst, stg[so][:],
                                        waits=[("dv", stn)])
                                gpump(1)

                wr_all = p.c("wr")
                E = prm["extent"]
                nkc = E // 512

                p.scope = "p3"
                mask_ld = p.dma("sync", "kv", mask_s, masks,
                                waits=[("pe", p.c("pe")), ("dv", p.c("dv")),
                                       ("gp", p.c("gp")), ("ac", p.c("ac"))])
                av_end = {}
                ld_h = {}
                negb_d = {}
                vt_d = {}
                out_guard = {}
                sum_guard = 0
                bank_guard = {}
                def loads(h, wpe):
                    hb = h % 2
                    p.dma("sync", "kv", kh[hb][:, 0:E], kss[h][:, 0:E],
                          waits=[("wr", wr_all), ("pe", wpe)])
                    p.dma("sync", "kv", vh[hb][:, 0:E], vss[h][:, 0:E])
                    ld_h[h] = p.dma("sync", "kv", qh[hb], qss[h])

                def bbound(h):
                    hb = h % 2
                    ld = ld_h[h]
                    p.op("scalar",
                         lambda sc: sc.activation(sqb[:, 0:E], kh[hb][:, 0:E],
                                                  AF.Square),
                         waits=[("kv", ld), ("pe", av_end.get(h - 1, 0))])
                    sq1 = p.c("ac")
                    for i in range(nkc):
                        p.op("tensor",
                             lambda t, i=i: t.matmul(
                                 pacc[2][0:1, :], onescol_b[:],
                                 sqb[:, i * 512:(i + 1) * 512],
                                 start=True, stop=True),
                             waits=[("ac", sq1), ("dv", p.c("dv"))], inc="pe")
                        mm = p.c("pe")
                        p.op("vector",
                             lambda v, i=i: v.reduce_max(mx8[:, i:i + 1],
                                                         pacc[2][0:1, :],
                                                         axis=AX.X),
                             waits=[("pe", mm)])
                    p.op("scalar",
                         lambda sc: sc.activation(sqb[:, 0:2048], qh[hb],
                                                  AF.Square),
                         waits=[("pe", p.c("pe"))])
                    sq2 = p.c("ac")
                    for i in range(4):
                        p.op("tensor",
                             lambda t, i=i: t.matmul(
                                 pacc[2][0:1, :], onescol_b[:],
                                 sqb[:, i * 512:(i + 1) * 512],
                                 start=True, stop=True),
                             waits=[("ac", sq2), ("dv", p.c("dv"))], inc="pe")
                        mm = p.c("pe")
                        p.op("vector",
                             lambda v, i=i: v.reduce_max(mx8[:, 8 + i:9 + i],
                                                         pacc[2][0:1, :],
                                                         axis=AX.X),
                             waits=[("pe", mm)])
                    p.op("vector", lambda v: v.reduce_max(sc1[:, 0:1],
                                                          mx8[:, 0:nkc],
                                                          axis=AX.X))
                    p.op("vector", lambda v: v.reduce_max(sc1[:, 1:2],
                                                          mx8[:, 8:12],
                                                          axis=AX.X))
                    p.op("vector", lambda v: v.tensor_tensor(
                        sc1[:, 2:3], sc1[:, 0:1], sc1[:, 1:2], op=ALU.mult))
                    prodd = p.c("dv")
                    p.op("scalar", lambda sc: sc.activation(sc1[:, 3:4],
                                                            sc1[:, 2:3],
                                                            AF.Ln),
                         waits=[("dv", prodd)])
                    p.op("scalar", lambda sc: sc.activation(sc1[:, 3:4],
                                                            sc1[:, 3:4],
                                                            AF.Exp,
                                                            scale=0.5))
                    sqd = p.c("ac")
                    p.op("vector", lambda v: v.tensor_scalar_mul(
                        sc1[:, 3:4], sc1[:, 3:4], -SCALE), waits=[("ac", sqd)])
                    nbd = p.c("dv")
                    p.op("tensor",
                         lambda t, hb=hb: t.matmul(pacc[2][:, 0:1], ones1f[:],
                                                   sc1[0:1, 3:4],
                                                   start=True, stop=True),
                         waits=[("dv", nbd)], inc="pe")
                    nbp = p.c("pe")
                    p.op("vector",
                         lambda v, hb=hb: v.tensor_copy(negb[hb][:],
                                                        pacc[2][:, 0:1]),
                         waits=[("pe", nbp)])
                    negb_d[h] = p.c("dv")
                    for cq in range(E // 512):
                        def vtf(t, cq=cq, hb=hb):
                            for j in range(4):
                                kc = cq * 4 + j
                                inst = t.transpose(
                                    ptr[:, j * 128:(j + 1) * 128],
                                    vh[hb][:, kc * 128:(kc + 1) * 128],
                                    idb_s[:])
                            return inst
                        p.op("tensor", vtf,
                             waits=[("kv", ld), ("dv", p.c("dv"))], inc="pe")
                        vt = p.c("pe")
                        p.op("vector",
                             lambda v, cq=cq, hb=hb: v.tensor_copy(
                                 vtok[hb][:, cq * 512:(cq + 1) * 512],
                                 ptr[:, 0:512]),
                             waits=[("pe", vt)])
                    vt_d[h] = p.c("dv")

                loads(0, 0)
                loads(1, 0)
                bbound(0)
                scrb = [psc[0], psc[1], pacc[3]]
                cool_wall = []
                for h in range(H):
                    hb = h % 2
                    if 1 <= h and h + 1 < H:
                        loads(h + 1, av_end[h - 1])
                    if h >= 1 and COOL_N > 0 and h % COOL_EVERY == 0:
                        cw = p.dma("sync", "cl", xn_tok[0:1, 0:2048],
                                   kss[0, 0:1, 0:2048],
                                   waits=[("pe", av_end[h - 1])])
                        for _ in range(COOL_N - 1):
                            cw = p.dma("sync", "cl", xn_tok[0:1, 0:2048],
                                       kss[0, 0:1, 0:2048])
                        cool_wall = [("cl", cw)]
                    for qi, (slot, pos) in enumerate(prm["qtiles"]):
                        if h + 1 < H and qi == 1:
                            bbound(h + 1)
                        nch = 4 * (pos + 1)
                        exps = [None] * nch
                        avs = [None] * nch
                        ob = (h * 4 + qi) % 2
                        for step in range(nch + 2):
                          if step < nch:
                            kc = step
                            bi3 = kc % 3
                            bank = scrb[bi3]
                            sw = ([("ac", exps[kc - 3])] if kc >= 3
                                  else list(bank_guard.get(bi3, [])) +
                                       [("dv", negb_d[h]), ("dv", vt_d[h])])
                            if kc == 0:
                                sw = sw + cool_wall
                                cool_wall = []
                            p.op("tensor",
                                 lambda t, kc=kc, slot=slot, hb=hb, bank=bank:
                                 t.matmul(
                                     bank[:],
                                     kh[hb][:, kc * 128:(kc + 1) * 128],
                                     qh[hb][:, slot * 512:(slot + 1) * 512],
                                     start=True, stop=True),
                                 waits=sw, inc="pe")
                            smm = p.c("pe")
                            wexp = [("pe", smm)]
                            if kc >= nch - 4:
                                r = kc - (nch - 4)
                                p.op("vector",
                                     lambda v, bank=bank, r=r: v.tensor_tensor(
                                         bank[:], bank[:],
                                         mask_s[:, r * 512:(r + 1) * 512],
                                         op=ALU.add),
                                     waits=[("pe", smm), ("kv", mask_ld)])
                                wexp = [("dv", p.c("dv"))]
                            if kc >= 4:
                                wexp.append(("pe", avs[kc - 4]))
                            p.op("scalar",
                                 lambda sc, kc=kc, bank=bank, hb=hb:
                                 sc.activation(
                                     pb[kc % 4][:], bank[:],
                                     AF.Exp, scale=SCALE, bias=negb[hb][:]),
                                 waits=wexp)
                            exps[kc] = p.c("ac")

                          if step >= 2:
                            kc = step - 2

                            def avpair(t, kc=kc, nch=nch, ob=ob, hb=hb):
                                t.matmul(pacc[ob][:],
                                         vtok[hb][:, kc * 128:(kc + 1) * 128],
                                         pb[kc % 4][:],
                                         start=(kc == 0),
                                         stop=(kc == nch - 1))
                                return t.matmul(prow[:], onescol_b[:],
                                                pb[kc % 4][:],
                                                start=(kc == 0),
                                                stop=(kc == nch - 1))
                            aw = [("ac", exps[kc])]
                            if kc == 0:
                                aw += [("dv", out_guard.get(ob, 0)),
                                       ("dv", sum_guard)]
                            p.op("tensor", avpair, waits=aw, inc="pe")
                            avs[kc] = p.c("pe")
                        av_end[h] = avs[nch - 1]
                        p.op("vector", lambda v: v.reciprocal(rs_sb[:],
                                                              prow[:]),
                             waits=[("pe", av_end[h])])
                        sum_guard = rsd = p.c("dv")
                        p.op("tensor",
                             lambda t: t.matmul(psc[1][:], ones1f[:],
                                                rs_sb[:],
                                                start=True, stop=True),
                             waits=[("dv", rsd)], inc="pe")
                        rbd = p.c("pe")
                        p.op("vector", lambda v: v.tensor_copy(r_sb[:],
                                                               psc[1][:]),
                             waits=[("pe", rbd)])
                        rsbd = p.c("dv")
                        for bi3 in range(3):
                            last = max(kc for kc in range(nch)
                                       if kc % 3 == bi3)
                            bank_guard[bi3] = [("ac", exps[last])]
                        bank_guard[1] = bank_guard[1] + [("dv", rsbd)]
                        oi = slot % 2
                        p.op("vector",
                             lambda v, oi=oi, ob=ob: v.scalar_tensor_tensor(
                                 ost[oi][:], pacc[ob][:], 1.0, r_sb[:],
                                 op0=ALU.mult, op1=ALU.mult),
                             waits=[("dv", rsbd),
                                    ("ao", ost_store.get(oi, 0))])
                        out_guard[ob] = p.c("dv")
                        ostd = p.c("dv")
                        ost_store[oi] = p.dma(
                            "gpsimd", "ao",
                            ag_scr[h * 128:(h + 1) * 128,
                                   slot * 512:(slot + 1) * 512],
                            ost[oi][:], waits=[("dv", ostd)])
                    gpump(2)
                gpump(999)
                p.drain_arm()
            p.arms_end()
            if DEBUG_STOP == 1:
                p.op("sync", lambda e: e.nop(), waits=[("ao", p.c("ao"))])
                p.emit(block, gs_s.ap()[0:1, 0:1], gsel_ready)
                return nc

            p.scope = "gfin"
            p.op("tensor", lambda t: t.matmul(prow[:, 0:2], onescol_f[:],
                                              gacc[:, 0:2], start=True,
                                              stop=True),
                 waits=[("dv", p.c("dv")), ("ac", p.c("ac")),
                        ("pe", p.c("pe"))], inc="pe")
            gpe = p.c("pe")
            p.op("vector", lambda v: v.tensor_copy(gsc[:, 0:2], prow[:, 0:2]),
                 waits=[("pe", gpe)])
            p.op("vector", lambda v: v.tensor_scalar_mul(gsc[:, 0:2],
                                                         gsc[:, 0:2],
                                                         1.0 / (D * FF)))
            p.op("vector", lambda v: v.tensor_scalar_add(gsc[:, 2:4],
                                                         gsc[:, 0:2], EPS_B))
            p.op("vector", lambda v: v.tensor_scalar_mul(gsc[:, 4:6],
                                                         gsc[:, 2:4], 0.5))
            p.op("vector", lambda v: v.reciprocal(gsc[:, 2:4], gsc[:, 2:4]))
            for (dst, col) in ((gfc_b, 0), (gpj_b, 1), (gfcinv_b, 2),
                               (gpjinv_b, 3), (thr_fc, 4), (thr_pj, 5)):
                p.op("tensor",
                     lambda t, col=col: t.matmul(pcor[:, 0:1], ones1f[:],
                                                 gsc[0:1, col:col + 1],
                                                 start=True, stop=True),
                     waits=[("dv", p.c("dv"))], inc="pe")
                mm = p.c("pe")
                p.op("vector", lambda v, dst=dst: v.tensor_copy(dst[:],
                                                                pcor[:, 0:1]),
                     waits=[("pe", mm)])
            gbc = p.c("dv")

            p.scope = "p5"
            wo_mm = {}
            attn_free = {}
            for tq in range(4):
                tb = tq * 512
                wag = p.dma("sync", "ag", agsl[:],
                            ag_scr[:, tb:tb + 512]
                            .rearrange("(a p) t -> p a t", p=128),
                            waits=[("pe", p.c("pe")), ("dv", p.c("dv")),
                                   ("ac", p.c("ac"))])
                for tsub in range(4):
                    xw = p.dma("sync", "sx2", xrow[tsub % 2],
                               xown[tb + tsub * 128: tb + (tsub + 1) * 128, :],
                               waits=[("dv", p.c("dv"))])
                    p.op("vector",
                         lambda v, tsub=tsub: v.tensor_copy(h1t[tsub],
                                                            xrow[tsub % 2]),
                         waits=[("sx2", xw), ("h1s", p.c("h1s"))])
                h1i = p.c("dv")
                for ec in range(16):
                    eb = ec % 2
                    wwo = p.dma("gpsimd", "wo", wob[eb],
                                woT[:, ec * 128:(ec + 1) * 128]
                                .rearrange("(a p) e -> p a e", p=128),
                                waits=[("pe", wo_mm.get(eb, 0))])

                    def wmm(t, eb=eb):
                        for dc in range(16):
                            inst = t.matmul(pacc[eb][:], wob[eb][:, dc, :],
                                            agsl[:, dc, :],
                                            start=(dc == 0), stop=(dc == 15))
                        return inst
                    p.op("tensor", wmm,
                         waits=[("wo", wwo), ("ag", wag),
                                ("ac", attn_free.get(eb, 0))], inc="pe")
                    wod = p.c("pe")
                    wo_mm[eb] = wod
                    p.op("scalar",
                         lambda sc, eb=eb: sc.activation(attn_sb[eb],
                                                         pacc[eb][:],
                                                         AF.Copy),
                         waits=[("pe", wod)])
                    attnd = p.c("ac")
                    attn_free[eb] = attnd

                    def trf5(t, eb=eb):
                        for tsub in range(4):
                            inst = t.transpose(
                                ptrf[:, tsub * 128:(tsub + 1) * 128],
                                attn_sb[eb][:, tsub * 128:(tsub + 1) * 128],
                                idf_s[:])
                        return inst
                    p.op("tensor", trf5,
                         waits=[("ac", attnd), ("dv", p.c("dv"))], inc="pe")
                    trd = p.c("pe")
                    for tsub in range(4):
                        p.op("vector",
                             lambda v, tsub=tsub, ec=ec: v.tensor_tensor(
                                 h1t[tsub][:, ec * 128:(ec + 1) * 128],
                                 h1t[tsub][:, ec * 128:(ec + 1) * 128],
                                 ptrf[:, tsub * 128:(tsub + 1) * 128],
                                 op=ALU.add),
                             waits=[("pe", trd), ("dv", h1i)])
                h1d = p.c("dv")
                for tsub in range(4):
                    t128 = tq * 4 + tsub
                    p.dma("scalar", "h1s",
                          xout[tb + tsub * 128: tb + (tsub + 1) * 128, :],
                          h1t[tsub], waits=[("dv", h1d)])
                    for ci in range(4):
                        p.op("vector",
                             lambda v, tsub=tsub, ci=ci: v.bn_stats(
                                 stats[:, ci * 6:(ci + 1) * 6],
                                 h1t[tsub][:, ci * 512:(ci + 1) * 512]),
                             waits=[("dv", h1d)])
                    p.op("vector", lambda v: v.bn_aggr(mv[:], stats[:, 0:24]))
                    agg = p.c("dv")
                    p.op("scalar", lambda sc: sc.activation(rstd[:],
                                                            mv[:, 1:2],
                                                            AF.Ln,
                                                            bias=epsa_t[:]),
                         waits=[("dv", agg)])
                    p.op("scalar", lambda sc: sc.activation(rstd[:], rstd[:],
                                                            AF.Exp,
                                                            scale=-0.5))
                    rst = p.c("ac")
                    p.op("vector",
                         lambda v, tsub=tsub: v.tensor_scalar(
                             xn2, h1t[tsub], mv[:, 0:1], rstd[:],
                             op0=ALU.subtract, op1=ALU.mult),
                         waits=[("ac", rst)])
                    p.op("vector", lambda v: v.reduce_max(
                        gq[:], xn2, axis=AX.X, apply_absolute_value=True))
                    p.op("vector", lambda v: v.reciprocal(s1v[:], gq[:]))
                    p.op("vector", lambda v: v.tensor_scalar_mul(
                        s1v[:], s1v[:], 128.0))
                    p.op("vector", lambda v: v.tensor_scalar_mul(
                        s2v[:], gq[:], 1.0 / 128.0))
                    p.op("vector", lambda v: v.tensor_scalar(
                        clipt, xn2, s1v[:], 128.0, op0=ALU.mult, op1=ALU.min))
                    p.op("vector", lambda v: v.tensor_scalar(
                        xh_tok, clipt, -128.0, s2v[:], op0=ALU.max,
                        op1=ALU.mult))
                    for dq in range(4):
                        def trx(t, dq=dq):
                            for j in range(4):
                                dc = dq * 4 + j
                                inst = t.transpose(
                                    ptr[:, j * 128:(j + 1) * 128],
                                    xh_tok[:, dc * 128:(dc + 1) * 128],
                                    idb_s[:])
                            return inst
                        p.op("tensor", trx, waits=[("dv", p.c("dv"))],
                             inc="pe")
                        trp = p.c("pe")
                        p.op("vector",
                             lambda v, dq=dq, t128=t128: v.tensor_copy(
                                 xfm2[:, dq * 4:(dq + 1) * 4,
                                      t128 * 128:(t128 + 1) * 128],
                                 _r3(ptr[:, 0:512], 128)),
                             waits=[("pe", trp)])
            xfm2d = p.c("dv")
            h1_stores = p.c("h1s")

            p.scope = "p6"
            fcm_done = {}
            gel_last = 0
            hs_done = {}

            def wf_load(oc, wv):
                wst = "wf" if oc % 2 == 0 else "wf2"
                weng = "sync" if oc % 2 == 0 else "scalar"
                return wst, p.dma(
                    weng, wst, wfs[oc % 3][:],
                    wfcT[:, oc * 128:(oc + 1) * 128]
                    .rearrange("(a p) n -> p a n", p=128),
                    waits=[("dv", wv)])
            wldd = {pc: wf_load(pc, xfm2d) for pc in range(3)}
            for oc in range(64):
                ob = oc % 2
                wb3 = oc % 3
                wst, wld = wldd[oc]
                wf = wfs[wb3].rearrange("p a b -> p (a b)")
                p.op("vector",
                     lambda v, wf=wf: v.tensor_scalar(
                         zfc2.rearrange("p a b -> p (a b)"), wf, gfcinv_b[:],
                         -0.5, op0=ALU.mult, op1=ALU.is_le),
                     waits=[(wst, wld), ("dv", gbc),
                            ("pe", p.c("pe") if oc < 2
                             else fcm_done.get(ob, 0))])
                p.op("vector",
                     lambda v, wf=wf, ob=ob: v.scalar_tensor_tensor(
                         tfc[ob].rearrange("p a b -> p (a b)"),
                         wf, thr_fc[:],
                         zfc2.rearrange("p a b -> p (a b)"),
                         op0=ALU.is_ge, op1=ALU.subtract))
                tern_done_oc = p.c("dv")
                if oc + 3 < 64:
                    wldd[oc + 3] = wf_load(oc + 3, tern_done_oc)

                fw = []
                if oc > 0 and oc % 8 == 0:
                    cw2v = p.dma("gpsimd", "cl2", xn_tok[0:1, 0:2048],
                                 kss[0, 0:1, 0:2048],
                                 waits=[("pe", fcm_done[1 - ob])])
                    cw2v = p.dma("gpsimd", "cl2", xn_tok[0:1, 0:2048],
                                 kss[0, 0:1, 0:2048])
                    fw = [("cl2", cw2v)]

                def fcm(t, ob=ob):
                    for dc in range(16):
                        for tq in range(4):
                            inst = t.matmul(pacc[tq][:], tfc[ob][:, dc, :],
                                            xfm2[:, dc,
                                                 tq * 512:(tq + 1) * 512],
                                            start=(dc == 0), stop=(dc == 15))
                    return inst
                p.op("tensor", fcm,
                     waits=[("dv", tern_done_oc), ("ac", gel_last)] + fw,
                     inc="pe")
                fcm_done[ob] = p.c("pe")
                for tq in range(4):
                    hb = tq % 2
                    p.op("scalar",
                         lambda sc, tq=tq, hb=hb: sc.activation(
                             hgel[hb][:], pacc[tq][:], AF.Gelu,
                             scale=gfc_b[:]),
                         waits=[("pe", fcm_done[ob]), ("dv", gbc),
                                ("hs", hs_done.get(hb, 0))])
                    gel_last = p.c("ac")
                    hs_done[hb] = p.dma(
                        "scalar", "hs",
                        h_scr[oc * 128:(oc + 1) * 128,
                              tq * 512:(tq + 1) * 512],
                        hgel[hb][:], waits=[("ac", gel_last)])
            h_stores = p.c("hs")

            p.scope = "p7a"
            ht_free_ac = {0: 0, 1: 0}
            ht_free_dv = {0: 0, 1: 0}
            std_prev = {0: 0, 1: 0}
            actdump = arfb.ap()[:, 4096:8192]
            for ti in range(16):
                tb2 = ti % 2
                hbuf = htok if tb2 == 0 else htok2
                hst = "ht" if tb2 == 0 else "ht2"
                w = p.dma("sync", hst, hbuf,
                          h_scr[:, ti * 128:(ti + 1) * 128],
                          waits=[("hs", h_stores),
                                 ("ac", ht_free_ac[tb2]),
                                 ("dv", ht_free_dv[tb2])] +
                                ([("pe", p.c("pe")), ("gp", p.c("gp")),
                                  ("dv", p.c("dv"))]
                                 if ti < 2 else []),
                          transpose=True)
                sc0 = 4 * tb2
                p.op("scalar", lambda sc, hbuf=hbuf, sc0=sc0: sc.activation(
                        actdump, hbuf[:, 0:4096], AF.Copy,
                        accum_out=stats[:, sc0:sc0 + 1]),
                     waits=[(hst, w), ("dv", std_prev[tb2])])
                p.op("scalar", lambda sc, hbuf=hbuf, sc0=sc0: sc.activation(
                        actdump, hbuf[:, 4096:8192], AF.Copy,
                        accum_out=stats[:, sc0 + 1:sc0 + 2]))
                p.op("scalar", lambda sc, hbuf=hbuf, sc0=sc0: sc.activation(
                        actdump, hbuf[:, 0:4096], AF.Square,
                        accum_out=stats[:, sc0 + 2:sc0 + 3]))
                p.op("scalar", lambda sc, hbuf=hbuf, sc0=sc0: sc.activation(
                        actdump, hbuf[:, 4096:8192], AF.Square,
                        accum_out=stats[:, sc0 + 3:sc0 + 4]))
                sums_done = p.c("ac")
                ht_free_ac[tb2] = sums_done
                p.op("vector", lambda v, hbuf=hbuf: v.reduce_max(
                        maxh[:], hbuf, axis=AX.X),
                     waits=[(hst, w)])
                p.op("vector", lambda v, hbuf=hbuf: v.tensor_reduce(
                        minh[:], hbuf, axis=AX.X, op=ALU.min))
                ht_free_dv[tb2] = p.c("dv")
                p.op("vector", lambda v, sc0=sc0: v.tensor_tensor(
                        mv[:, 0:1], stats[:, sc0:sc0 + 1],
                        stats[:, sc0 + 1:sc0 + 2], op=ALU.add),
                     waits=[("ac", sums_done)])
                p.op("vector", lambda v: v.tensor_scalar_mul(
                        mv[:, 0:1], mv[:, 0:1], 1.0 / FF))
                p.op("vector", lambda v, sc0=sc0: v.tensor_tensor(
                        mv[:, 1:2], stats[:, sc0 + 2:sc0 + 3],
                        stats[:, sc0 + 3:sc0 + 4], op=ALU.add))
                p.op("vector", lambda v: v.tensor_scalar_mul(
                        mv[:, 1:2], mv[:, 1:2], 1.0 / FF))
                p.op("vector", lambda v: v.tensor_tensor(
                        ga[:], mv[:, 0:1], mv[:, 0:1], op=ALU.mult))
                p.op("vector", lambda v: v.tensor_tensor(
                        mv[:, 1:2], mv[:, 1:2], ga[:], op=ALU.subtract))
                std_prev[tb2] = p.c("dv")
                p.op("scalar", lambda sc: sc.activation(rstd[:], mv[:, 1:2],
                                                        AF.Ln,
                                                        bias=epsb_t[:]),
                     waits=[("dv", std_prev[tb2])])
                p.op("scalar", lambda sc: sc.activation(rstd[:], rstd[:],
                                                        AF.Exp, scale=-0.5))
                rst = p.c("ac")
                p.op("vector", lambda v: v.tensor_scalar(
                    ga[:], maxh[:], mv[:, 0:1], None, op0=ALU.subtract),
                     waits=[("ac", rst)])
                p.op("vector", lambda v: v.tensor_scalar(
                    gb[:], minh[:], mv[:, 0:1], -1.0, op0=ALU.subtract,
                    op1=ALU.mult))
                p.op("vector", lambda v: v.tensor_tensor(gq[:], ga[:], gb[:],
                                                         op=ALU.max))
                p.op("vector", lambda v: v.tensor_scalar(gq[:], gq[:],
                                                         rstd[:], None,
                                                         op0=ALU.mult))
                p.op("vector", lambda v: v.reciprocal(s1v[:], gq[:]))
                p.op("vector", lambda v: v.tensor_scalar_mul(s1v[:], s1v[:],
                                                             128.0))
                p.op("vector", lambda v: v.tensor_scalar_mul(s2v[:], gq[:],
                                                             1.0 / 128.0))
                p.op("vector", lambda v: v.tensor_scalar(s1v[:], s1v[:],
                                                         s2v[:], rstd[:],
                                                         op0=ALU.mult,
                                                         op1=ALU.mult))
                p.op("vector",
                     lambda v, ti=ti: v.tensor_scalar(
                         svec[:, ti:ti + 1], s1v[:], gpj_b[:], None,
                         op0=ALU.mult),
                     waits=[("dv", gbc)])
                p.op("vector",
                     lambda v, ti=ti: v.tensor_scalar(
                         msv[:], svec[:, ti:ti + 1], mv[:, 0:1], -1.0,
                         op0=ALU.mult, op1=ALU.mult))
                msd = p.c("dv")
                p.op("tensor", lambda t: t.matmul(prow[:, 0:128], msv[:],
                                                  idf_s[:],
                                                  start=True, stop=True),
                     waits=[("dv", msd)], inc="pe")
                msp = p.c("pe")
                p.op("vector",
                     lambda v, ti=ti: v.tensor_copy(
                         msrow[:, ti * 128:(ti + 1) * 128], prow[:, 0:128]),
                     waits=[("pe", msp)])
            p7a_d = p.c("dv")

            p.scope = "p7b"
            o2_done = {}
            o2_all = 0
            wp_done = {}
            for eb in range(4):
                ebc = eb * 512
                for half in range(8):
                    hb3 = half % 3
                    wst = "wp" if half % 2 == 0 else "wp2"
                    weng = "sync" if half % 2 == 0 else "scalar"
                    exw = []
                    if eb == 0 and half < 2:
                        exw = [("h1s", h1_stores)]
                    elif eb == 0 and half == 2:
                        exw = [("hs", h_stores), ("ac", p.c("ac")),
                               ("gp", p.c("gp"))]
                    wld = p.dma(weng, wst, wps[hb3],
                                wpjT[half * 1024:(half + 1) * 1024,
                                     ebc:ebc + 512]
                                .rearrange("(a p) n -> p a n", p=128),
                                waits=[("dv", wp_done.get(hb3, 0))] + exw)
                    wfp = wps[hb3].rearrange("p a b -> p (a b)")
                    p.op("vector",
                         lambda v, wfp=wfp: v.tensor_scalar(
                             zpj2.rearrange("p a b -> p (a b)"), wfp,
                             gpjinv_b[:], -0.5, op0=ALU.mult, op1=ALU.is_le),
                         waits=[(wst, wld), ("dv", gbc),
                                ("pe", p.c("pe") if half < 3 else 0)])
                    p.op("vector",
                         lambda v, wfp=wfp, half=half:
                         v.scalar_tensor_tensor(
                             tpj[:, half * 8:(half + 1) * 8, :].rearrange(
                                 "p a b -> p (a b)"),
                             wfp, thr_pj[:],
                             zpj2.rearrange("p a b -> p (a b)"),
                             op0=ALU.is_ge, op1=ALU.subtract))
                    wp_done[hb3] = p.c("dv")
                ternd = p.c("dv")

                def csf(t):
                    for oc in range(64):
                        inst = t.matmul(prow[:], onescol_b[:], tpj[:, oc, :],
                                        start=(oc == 0), stop=(oc == 63))
                    return inst
                p.op("tensor", csf, waits=[("dv", ternd)], inc="pe")
                csd = p.c("pe")
                p.op("vector", lambda v: v.tensor_copy(crow[:], prow[:]),
                     waits=[("pe", csd)])
                crd = p.c("dv")

                for tq in range(4):
                    hwv = [None] * 64
                    pe_oc = [None] * 64
                    for pre in range(3):
                        hwv[pre] = p.dma(
                            "gpsimd", f"htl{pre}", htile[pre][:],
                            h_scr[pre * 128:(pre + 1) * 128,
                                  tq * 512:(tq + 1) * 512],
                            waits=[("hs", h_stores)])
                    pw = []
                    if (eb * 4 + tq) % 2 == 0 and eb + tq > 0:
                        cw2v = p.dma("gpsimd", "cl2", xn_tok[0:1, 0:2048],
                                     kss[0, 0:1, 0:2048],
                                     waits=[("pe", p.c("pe"))])
                        cw2v = p.dma("gpsimd", "cl2", xn_tok[0:1, 0:2048],
                                     kss[0, 0:1, 0:2048])
                        pw = [("cl2", cw2v)]
                    for oc in range(64):
                        def pjm(t, oc=oc):
                            for ts4 in range(4):
                                inst = t.matmul(
                                    pacc[ts4][:],
                                    htile[oc % 3][:,
                                                  ts4 * 128:(ts4 + 1) * 128],
                                    tpj[:, oc, :],
                                    start=(oc == 0), stop=(oc == 63))
                            return inst
                        p.op("tensor", pjm,
                             waits=[(f"htl{oc % 3}", hwv[oc]),
                                    ("dv", o2_all if oc == 0 else 0)] +
                                   (pw if oc == 0 else []),
                             inc="pe")
                        pe_oc[oc] = p.c("pe")
                        if oc + 3 < 64:
                            hwv[oc + 3] = p.dma(
                                "gpsimd", f"htl{oc % 3}", htile[oc % 3][:],
                                h_scr[(oc + 3) * 128:(oc + 4) * 128,
                                      tq * 512:(tq + 1) * 512],
                                waits=[("pe", pe_oc[oc])])
                    pjd = p.c("pe")
                    for ts4 in range(4):
                        t128 = tq * 4 + ts4
                        p.op("tensor",
                             lambda t, t128=t128: t.matmul(
                                 pcor[:],
                                 msrow[0:1, t128 * 128:(t128 + 1) * 128],
                                 crow[:], start=True, stop=True),
                             waits=[("dv", p.c("dv")), ("pe", pjd)], inc="pe")
                        cord = p.c("pe")
                        p.op("vector", lambda v: v.tensor_copy(csb[:],
                                                               pcor[:]),
                             waits=[("pe", cord)])
                        csbd = p.c("dv")
                        oi = ts4 % 2
                        p.op("vector",
                             lambda v, ts4=ts4, t128=t128, oi=oi:
                             v.scalar_tensor_tensor(
                                 o2sb[oi][:], pacc[ts4][:],
                                 svec[:, t128:t128 + 1], csb[:],
                                 op0=ALU.mult, op1=ALU.add),
                             waits=[("dv", csbd),
                                    ("o2s", o2_done.get(oi, 0)),
                                    ("pe", pjd)])
                        o2d = p.c("dv")
                        o2_done[oi] = p.dma(
                            "gpsimd", "o2s",
                            xout[t128 * 128:(t128 + 1) * 128, ebc:ebc + 512],
                            o2sb[oi][:],
                            waits=[("dv", o2d), ("h1s", h1_stores)],
                            accum_op=ALU.add)
                    o2_all = p.c("dv")

            p.op("sync", lambda e: e.nop(),
                 waits=[("o2s", p.c("o2s")), ("h1s", h1_stores),
                        ("hs", p.c("hs"))])
            if p.c("cl2") > 0:
                p.op("gpsimd", lambda e: e.nop(),
                     waits=[("cl2", p.c("cl2"))], _isdma=True)
            p.emit(block, gs_s.ap()[0:1, 0:1], gsel_ready)
    return nc



def _host_constants():
    inv = (np.float32(1.0) /
           np.float32(10000.0) ** (np.arange(0, HD, 2, dtype=np.float32) /
                                   np.float32(HD)))
    t = np.arange(S, dtype=np.float32)
    freqs = np.outer(t, inv).astype(np.float32)
    emb = np.concatenate([freqs, freqs], axis=1)
    cosf = np.ascontiguousarray(np.cos(emb).T.astype(np.float32))
    sinf = np.ascontiguousarray(np.sin(emb).T.astype(np.float32))
    rt = np.zeros((HD, HD), dtype=ml_dtypes.bfloat16)
    for i in range(64):
        rt[i + 64, i] = -1
        rt[i, i + 64] = 1
    idb = np.eye(128, dtype=ml_dtypes.bfloat16)
    idf = np.eye(128, dtype=np.float32)
    masks = np.zeros((128, 4 * 512), dtype=np.float32)
    pp = np.arange(128)[:, None]
    nn = np.arange(512)[None, :]
    for r in range(4):
        masks[:, r * 512:(r + 1) * 512] = np.where(r * 128 + pp <= nn, 0.0, NEG)
    return cosf, sinf, rt, idb, idf, masks


_NC_CACHE = {}
DEBUG_STOP = 0
COOL_N = 3
COOL_EVERY = 1


def kernel(L0_attn_x, L0_attn_Wq, L0_attn_Wk, L0_attn_Wv, L0_attn_Wo,
           L0_ffn_w_fc, L0_ffn_w_proj):
    x = np.ascontiguousarray(np.asarray(L0_attn_x, dtype=np.float32))
    Wq = np.asarray(L0_attn_Wq, dtype=np.float32)
    Wk = np.asarray(L0_attn_Wk, dtype=np.float32)
    Wv = np.asarray(L0_attn_Wv, dtype=np.float32)
    Wo = np.asarray(L0_attn_Wo, dtype=np.float32)
    wfc = np.asarray(L0_ffn_w_fc, dtype=np.float32)
    wpj = np.asarray(L0_ffn_w_proj, dtype=np.float32)
    cosf, sinf, rt, idb, idf, masks = _host_constants()

    if "nc" not in _NC_CACHE:
        _NC_CACHE["nc"] = build_fused()
    nc = _NC_CACHE["nc"]

    wqTh = np.ascontiguousarray(Wq.T)
    wkTh = np.ascontiguousarray(Wk.T)
    wvTh = np.ascontiguousarray(Wv.T)
    woTh = np.ascontiguousarray(Wo.T)
    wfcTh = np.ascontiguousarray(wfc.T)
    wpjTh = np.ascontiguousarray(wpj.T)
    in_maps = []
    for c in range(8):
        b, g = c // 2, c % 2
        if g == 0:
            xo = np.concatenate([x[b, 0:1024], x[b, 3072:4096]], axis=0)
        else:
            xo = x[b, 1024:3072]
        in_maps.append({
            "xb": x[b],
            "xown": np.ascontiguousarray(xo),
            "wqT": wqTh, "wkT": wkTh, "wvT": wvTh, "woT": woTh,
            "wfcT": wfcTh, "wpjT": wpjTh,
            "cosf": cosf, "sinf": sinf, "rtc": rt, "idbi": idb, "idfi": idf,
            "masks": masks,
            "gseli": np.array([[g]], dtype=np.int32),
        })
    res = run_bass_kernel_spmd(nc, in_maps, core_ids=list(range(8)))
    out = np.empty((B, S, D), dtype=np.float32)
    for c in range(8):
        b, g = c // 2, c % 2
        xo = res.results[c]["xout"]
        if g == 0:
            out[b, 0:1024] = xo[0:1024]
            out[b, 3072:4096] = xo[1024:2048]
        else:
            out[b, 1024:3072] = xo
    return out
